# version 08
# baseline (speedup 1.0000x reference)
import os
import numpy as np
import ml_dtypes
import concourse.bass as bass
import concourse.mybir as mybir
from concourse.bass_utils import run_bass_kernel_spmd

F32 = mybir.dt.float32
BF16 = mybir.dt.bfloat16
AF = mybir.ActivationFunctionType
ALU = mybir.AluOpType
MAGIC = 12582912.0

D = 1024
FC = 8
H = 16
HD = 64
CHUNK = 256
E = 8
DFF = 4096
FADA = 4608
NCORE = 8

ENGINES = ("sp", "pool", "pe", "act", "dve")
_ENG_ATTR = {"sp": "sync", "pool": "gpsimd", "pe": "tensor", "act": "scalar",
             "dve": "vector"}


class Ticket:
    __slots__ = ("engine", "sem", "value", "seq", "is_dma")

    def __init__(self, engine, sem, value, seq, is_dma=False):
        self.engine = engine
        self.sem = sem
        self.value = value
        self.seq = seq
        self.is_dma = is_dma


class Prog:
    def __init__(self, nc):
        self.nc = nc
        self.steps = {e: [] for e in ENGINES}
        self.sems = {}
        self.counts = {}
        self.seq = {e: 0 for e in ENGINES}
        self.hwm = {e: {} for e in ENGINES}
        self._sem_ctxs = []

    NDMA = 8

    def make_sems(self):
        for e in ENGINES:
            ctx = self.nc.semaphore(f"sem_{e}_op")
            self.sems[(e, "op")] = ctx.__enter__()
            self._sem_ctxs.append(ctx)
        self.dma_sems = {}
        self.dma_uses = {}
        self.dma_rot = {}
        for e in ("sp", "pool", "act"):
            self.dma_sems[e] = []
            for i in range(self.NDMA):
                ctx = self.nc.semaphore(f"dma_{e}_{i}")
                self.dma_sems[e].append(ctx.__enter__())
                self._sem_ctxs.append(ctx)
            self.dma_uses[e] = [0] * self.NDMA
            self.dma_rot[e] = 0

    def add(self, engine, fns, deps=(), inc=1, tick=True):
        assert inc == 1
        if callable(fns):
            fns = [fns]
        waits = self._mk_waits(engine, deps)
        sem = self.sems[(engine, "op")]
        ckey = (engine, "op")
        if ckey not in self.counts:
            self.counts[ckey] = 0
        if tick:
            self.counts[ckey] += inc
        value = self.counts[ckey]
        self.seq[engine] += 1
        self.steps[engine].append((waits, fns, sem if tick else None, inc))
        return Ticket(engine, sem, value, self.seq[engine], is_dma=False)

    NO_ELIDE = os.environ.get("KNOELIDE", "") == "1"

    def _mk_waits(self, engine, deps):
        waits = []
        hw = self.hwm[engine]
        best = {}
        for d in deps:
            if d is None:
                continue
            if (not self.NO_ELIDE) and hw.get(id(d.sem), -1) >= d.value:
                continue
            k = id(d.sem)
            if k in best and best[k][1] >= d.value:
                continue
            best[k] = (d.sem, d.value)
            hw[k] = max(hw.get(k, -1), d.value)
        return list(best.values())

    def dma(self, engine, fns, deps=()):
        assert engine in ("sp", "pool", "act")
        if callable(fns):
            fns = [fns]
        k = self.dma_rot[engine]
        self.dma_rot[engine] = (k + 1) % self.NDMA
        sem = self.dma_sems[engine][k]
        prev = self.dma_uses[engine][k] * 16
        waits = self._mk_waits(engine, deps)
        if prev > 0 and self.hwm[engine].get(id(sem), -1) < prev:
            waits.append((sem, prev))
            self.hwm[engine][id(sem)] = prev
        self.dma_uses[engine][k] += 1
        value = self.dma_uses[engine][k] * 16
        self.seq[engine] += 1
        self.steps[engine].append((waits, fns, sem, 16))
        return Ticket(engine, sem, value, self.seq[engine], is_dma=True)

    def now(self):
        out = []
        for (e, kind), cnt in self.counts.items():
            if cnt > 0:
                out.append(Ticket(e, self.sems[(e, kind)], cnt, self.seq[e]))
        for e in self.dma_sems:
            for i, sem in enumerate(self.dma_sems[e]):
                if self.dma_uses[e][i] > 0:
                    out.append(Ticket(e, sem, self.dma_uses[e][i] * 16,
                                      self.seq[e], is_dma=True))
        return out

    def barrier(self):
        cur = self.now()
        outs = {}
        for e in ENGINES:
            outs[e] = self.add(e, lambda eng: eng.nop(), deps=cur, inc=1)
        return outs

    def build(self, block):
        def mk(engine):
            def body(eng):
                for waits, fns, sem, inc in self.steps[engine]:
                    for s, v in waits:
                        eng.wait_ge(s, v)
                    inst = None
                    for f in fns:
                        inst = f(eng)
                    if sem is not None:
                        inst.then_inc(sem, inc)
            return body
        for e in ENGINES:
            if self.steps[e]:
                getattr(block, _ENG_ATTR[e])(mk(e))

    def final_waits(self):
        return self.now()


class Arena:

    def __init__(self, tensor, cols):
        self.t = tensor
        self.cols = cols
        self.off = 0
        self.marks = []

    def alloc(self, n, dtype=F32):
        nf = n if dtype == F32 else (n + 1) // 2
        assert self.off + nf <= self.cols, f"arena overflow {self.off}+{nf}>{self.cols}"
        ap = self.t[:, self.off:self.off + nf]
        self.off += nf
        if dtype != F32:
            ap = ap.bitcast(BF16)
        return ap

    def push(self):
        self.marks.append(self.off)

    def pop(self):
        self.off = self.marks.pop()


def host_constants(L):
    c = {}
    c["ones_f"] = np.ones((128, 128), dtype=np.float32)
    c["ones_b"] = np.ones((128, 128), dtype=ml_dtypes.bfloat16)
    c["ident_f"] = np.eye(128, dtype=np.float32)
    c["ident_b"] = np.eye(128, dtype=ml_dtypes.bfloat16)
    e01 = np.zeros((128, 1), dtype=np.float32)
    e01[0, 0] = 1.0
    e01[1, 0] = 1.0
    c["e01"] = e01
    c["iota8c"] = np.arange(128, dtype=np.float32).reshape(128, 1)
    c["c8c"] = ((np.arange(128) - 1) % 8).astype(np.float32).reshape(128, 1)
    sel = np.zeros((8, 8 * 128), dtype=np.float32)
    for e in range(8):
        sel[e, e * 128:(e + 1) * 128] = 1.0
    c["sel8"] = sel.astype(ml_dtypes.bfloat16)
    pos = np.arange(L, dtype=np.float64)
    inv_freq = 1.0 / (10000.0 ** (np.arange(0, HD, 2, dtype=np.float64) / HD))
    freqs = pos[:, None] * inv_freq[None, :]
    emb = np.concatenate([freqs, freqs], axis=-1)
    cos = np.cos(emb).T.astype(np.float32)
    sin = np.sin(emb).T.astype(np.float32)
    c["cos2"] = np.concatenate([cos, cos], 0).astype(ml_dtypes.bfloat16)
    c["sin2"] = np.concatenate([sin, sin], 0).astype(ml_dtypes.bfloat16)
    P64 = np.zeros((64, 64), dtype=np.float32)
    for i in range(32):
        P64[2 * i + 1, 2 * i] = -1.0
        P64[2 * i, 2 * i + 1] = 1.0
    P128 = np.zeros((128, 128), dtype=np.float32)
    P128[:64, :64] = P64
    P128[64:, 64:] = P64
    c["rotP"] = P128.astype(ml_dtypes.bfloat16)
    SH = np.zeros((128, 64), dtype=np.float32)
    for m in range(64):
        SH[m + 64, m] = 1.0
    c["shift64"] = SH.astype(ml_dtypes.bfloat16)
    SU = np.zeros((64, 128), dtype=np.float32)
    for d in range(64):
        SU[d, d + 64] = 1.0
    c["shiftup"] = SU.astype(ml_dtypes.bfloat16)
    m0 = (np.arange(128)[:, None] <= np.arange(256)[None, :]).astype(np.float32)
    m1 = ((np.arange(128) + 128)[:, None] <= np.arange(256)[None, :]).astype(np.float32)
    c["tril"] = np.concatenate([m0, m1], axis=1).astype(ml_dtypes.bfloat16)
    c["iota36"] = (np.arange(36)[None, :] * 128
                   + np.arange(128)[:, None]).astype(np.float32)
    c["cvals"] = np.repeat(np.array([[1e-5, 1e-6, 1.0]], dtype=np.float32),
                           128, axis=0)
    return c


def norm_stats(ctx, x_fc, pb1, pb2, scr_a, scr_b, out_mbc, out_rbc, deps_in,
               eps, mean=True):
    P = ctx["P"]
    if not isinstance(deps_in, (list, tuple)):
        deps_in = [deps_in]
    deps_in = list(deps_in)
    ones = ctx["sc"]["ones_f"]
    t1 = None
    if mean:
        def mk(fc):
            return lambda e: e.matmul(pb1, ones, x_fc(fc),
                                      start=(fc == 0), stop=(fc == FC - 1))
        t1 = P.add("pe", [mk(fc) for fc in range(FC)], deps=deps_in)
    tsq_prev = [None, None]
    tmm = None
    for fc in range(FC):
        scr = scr_a if fc % 2 == 0 else scr_b
        ta = P.add("act", lambda e, s=x_fc(fc), d=scr: e.activation(d, s, AF.Square),
                   deps=deps_in + [tsq_prev[fc % 2]])
        tmm = P.add("pe", lambda e, d=scr, fc=fc: e.matmul(pb2, ones, d,
                    start=(fc == 0), stop=(fc == FC - 1)),
                    deps=[ta] + ([t1] if t1 is not None else []))
        tsq_prev[fc % 2] = tmm
    invD = 1.0 / D
    if mean:
        tmb = P.add("dve", lambda e: e.tensor_scalar(out_mbc, pb1, invD, None,
                    ALU.mult), deps=[t1, tmm])
        tv1 = P.add("dve", lambda e: e.tensor_tensor(out_rbc, out_mbc, out_mbc,
                    ALU.mult), deps=[tmb])
        tv2 = P.add("dve", lambda e: e.scalar_tensor_tensor(out_rbc, pb2, invD,
                    out_rbc, ALU.mult, ALU.subtract), deps=[tv1, tmm])
        t_s_read = tv2
    else:
        tmb = None
        tv2 = P.add("dve", lambda e: e.tensor_scalar(out_rbc, pb2, invD, None,
                    ALU.mult), deps=[tmm])
        t_s_read = tv2
    epscol = ctx["sc"]["cvals"][:, 0:1] if eps == 1e-5 else ctx["sc"]["cvals"][:, 1:2]
    tl = P.add("act", lambda e: e.activation(out_rbc, out_rbc, AF.Ln, bias=epscol),
               deps=[tv2])
    te = P.add("act", lambda e: e.activation(out_rbc, out_rbc, AF.Exp, scale=-0.5),
               deps=[tl])
    return te, tmb, t_s_read


def elu1(ctx, src_ps, dst_bf, scr_a, scr_b, dep, extra=()):
    P = ctx["P"]
    t1 = P.add("dve", lambda e: e.tensor_scalar(scr_a, src_ps, 0.0, None, ALU.min),
               deps=[dep] + list(extra))
    t2 = P.add("act", lambda e: e.activation(scr_b, scr_a, AF.Exp), deps=[t1])
    t3 = P.add("dve", lambda e: e.scalar_tensor_tensor(dst_bf, src_ps, 0.0, scr_b,
               ALU.max, ALU.add), deps=[t2])
    return t3


def emit_m8(ctx, s01, m8, scr, deps):
    P = ctx["P"]
    t0 = P.add("act", lambda e: e.activation(m8, s01, AF.Abs), deps=deps)
    t1 = P.add("act", lambda e: e.activation(scr, s01, AF.Sign), deps=[t0])
    t2 = P.add("dve", lambda e: e.tensor_scalar(m8, m8, 0.5, MAGIC, ALU.subtract,
               ALU.add), deps=[t0])
    t3 = P.add("dve", lambda e: e.tensor_scalar(m8, m8, MAGIC, None, ALU.subtract),
               deps=[t2])
    t4 = P.add("dve", lambda e: e.tensor_tensor(m8, m8, scr, ALU.mult),
               deps=[t3, t1])
    t5 = P.add("dve", lambda e: e.tensor_scalar(scr, m8, 0.125, 0.4375, ALU.mult,
               ALU.subtract), deps=[t4])
    t6 = P.add("dve", lambda e: e.tensor_scalar(scr, scr, MAGIC, MAGIC, ALU.add,
               ALU.subtract), deps=[t5])
    t7 = P.add("dve", lambda e: e.tensor_scalar(scr, scr, -8.0, None, ALU.mult),
               deps=[t6])
    t8 = P.add("dve", lambda e: e.tensor_tensor(m8, m8, scr, ALU.add), deps=[t7])
    return t8


def emit_tin(ctx):
    P, AR, sc, PB = ctx["P"], ctx["AR"], ctx["sc"], ctx["PB"]
    NT512 = ctx["NT512"]
    x_in, xs = ctx["x_in"], ctx["xs"]
    xsr = xs.rearrange("(f p) t -> p f t", p=128)
    AR.push()
    xtm = [AR.alloc(4 * D) for _ in range(2)]
    xfm = [AR.alloc(FC * 512) for _ in range(2)]
    ident = sc["ident_f"]
    t_tp_last = [None, None]
    t_st_last = [None, None]
    t_ev = {0: None, 1: None}
    for t in range(NT512):
        bi = t % 2
        buf = xtm[bi].rearrange("p (s d) -> p s d", s=4)
        tl = P.dma("sp", lambda e, b=buf, t=t: e.dma_start(
            out=b, in_=x_in[t * 512:(t + 1) * 512, :].rearrange(
                "(s p) d -> p s d", p=128)), deps=[t_tp_last[bi]])
        fb = xfm[bi].rearrange("p (f t) -> p f t", f=FC)
        evs = []
        for fc in range(FC):
            bk = fc % 2
            fns = [(lambda e, s=s, fc=fc, b=buf, bk=bk: e.transpose(
                PB[bk][:, s * 128:(s + 1) * 128],
                b[:, s, fc * 128:(fc + 1) * 128], ident)) for s in range(4)]
            tm = P.add("pe", fns, deps=[tl, t_ev[bk]])
            if fc == FC - 1:
                t_tp_last[bi] = tm
            t_ev[bk] = P.add("dve", lambda e, fc=fc, bk=bk, fb=fb: e.tensor_copy(
                fb[:, fc, :], PB[bk][:, 0:512]), deps=[tm, t_st_last[bi]])
            evs.append(t_ev[bk])
        t_st_last[bi] = P.dma("sp", lambda e, fb=fb, t=t: e.dma_start(
            out=xsr[:, :, t * 512:(t + 1) * 512], in_=fb), deps=evs)
    AR.pop()


def emit_tout(ctx):
    P, AR, sc, PB = ctx["P"], ctx["AR"], ctx["sc"], ctx["PB"]
    NT512 = ctx["NT512"]
    xs, y_out = ctx["xs"], ctx["y_out"]
    xsr = xs.rearrange("(f p) t -> p f t", p=128)
    AR.push()
    xfm = [AR.alloc(FC * 512) for _ in range(2)]
    ytm = [AR.alloc(4 * D) for _ in range(2)]
    ident = sc["ident_f"]
    t_rd_last = [None, None]
    t_st_last = [None, None]
    t_ev = {0: None, 1: None}
    for t in range(NT512):
        bi = t % 2
        fb = xfm[bi].rearrange("p (f t) -> p f t", f=FC)
        tl = P.dma("sp", lambda e, fb=fb, t=t: e.dma_start(
            out=fb, in_=xsr[:, :, t * 512:(t + 1) * 512]), deps=[t_rd_last[bi]])
        ob = ytm[bi].rearrange("p (s d) -> p s d", s=4)
        evs = []
        for s in range(4):
            for half in range(2):
                bk = (2 * s + half) % 2
                fns = [(lambda e, fc=fc, s=s, fb=fb, bk=bk: e.transpose(
                    PB[bk][:, (fc % 4) * 128:(fc % 4 + 1) * 128],
                    fb[:, fc, s * 128:(s + 1) * 128], ident))
                    for fc in range(half * 4, half * 4 + 4)]
                tm = P.add("pe", fns, deps=[tl, t_ev[bk]])
                if s == 3 and half == 1:
                    t_rd_last[bi] = tm
                t_ev[bk] = P.add("dve", lambda e, s=s, half=half, ob=ob, bk=bk:
                                 e.tensor_copy(ob[:, s, half * 512:(half + 1) * 512],
                                               PB[bk][:, 0:512]),
                                 deps=[tm, t_st_last[bi]])
                evs.append(t_ev[bk])
        t_st_last[bi] = P.dma("sp", lambda e, ob=ob, t=t: e.dma_start(
            out=y_out[t * 512:(t + 1) * 512, :].rearrange("(s p) d -> p s d", p=128),
            in_=ob), deps=evs)
    AR.pop()


def emit_moe(ctx):
    P, AR, sc, PB = ctx["P"], ctx["AR"], ctx["sc"], ctx["PB"]
    LH, NHT = ctx["LH"], ctx["NHT"]
    xs, fW1, fW2, b1c, fb2 = ctx["xs"], ctx["fW1"], ctx["fW2"], ctx["b1c"], ctx["fb2"]
    b2pk = ctx["b2pk"]
    xsr = xs.rearrange("(f p) t -> p f t", p=128)
    NQT = LH // 256

    AR.push()
    x_half = AR.alloc(FC * LH).rearrange("p (f t) -> p f t", f=FC)
    ln2 = AR.alloc(FC * LH, BF16).rearrange("p (f t) -> p f t", f=FC)
    b1sb = AR.alloc(E * 32)
    w1b = [AR.alloc(FC * 1024, BF16).rearrange("p (f q) -> p f q", f=FC)
           for _ in range(2)]
    w2b = [AR.alloc(FC * 1024, BF16).rearrange("p (d q) -> p d q", d=FC)
           for _ in range(2)]
    hsb = [AR.alloc(512, BF16) for _ in range(8)]
    hsc = [AR.alloc(512, BF16) for _ in range(3)]
    scr = [AR.alloc(512) for _ in range(2)]
    asc = [AR.alloc(512), None]
    mbc = AR.alloc(512)
    rbc = AR.alloc(512)
    wbc = AR.alloc((LH // 512) * 512, BF16)
    s01r = AR.alloc(512)[0:1, :]
    m8r = AR.alloc(512)[0:1, :]
    wm8 = AR.alloc(LH, BF16)[:8, :]
    wsc = AR.alloc(512)
    wsc2 = AR.alloc(512)
    rscr = wsc2[0:1, :]

    tb1 = P.dma("sp", lambda e: e.dma_start(out=b1sb, in_=b1c[:]))

    for half in range(2):
        hsl = slice(half * LH, (half + 1) * LH)
        t_loads = []
        for t4i in range(NHT):
            t_loads.append(P.dma("sp", lambda e, half=half, t4i=t4i:
                e.dma_start(out=x_half[:, :, t4i * 512:(t4i + 1) * 512],
                            in_=xsr[:, :, half * LH + t4i * 512:
                                    half * LH + (t4i + 1) * 512])))
        tload = t_loads[-1]
        t_lnapply = [None, None]
        t_routing = []
        t_pb67 = t_loads[0]
        t_m8rd = None
        for t4i in range(NHT):
            tsl = slice(t4i * 512, (t4i + 1) * 512)
            te, tmb, tsr = norm_stats(
                ctx, lambda fc, tsl=tsl: x_half[:, fc, tsl], PB[6][:, 0:512],
                PB[7][:, 0:512], scr[0], scr[1], mbc, rbc,
                [t_loads[t4i], t_pb67], 1e-5, mean=True)
            tp = P.add("pe", lambda e, tsl=tsl: e.matmul(
                PB[6][0:1, 0:512], sc["e01"], x_half[:, 0, tsl],
                start=True, stop=True),
                deps=[tmb, tsr, t_routing[-1] if t_routing else None,
                      t_loads[t4i]])
            ta = P.add("dve", lambda e: e.tensor_scalar(
                s01r, mbc[0:1, 0:512], -2.0, None, ALU.mult),
                deps=[tmb, t_routing[-1] if t_routing else None])
            tb = P.add("dve", lambda e: e.tensor_tensor(
                s01r, PB[6][0:1, 0:512], s01r, ALU.add), deps=[ta, tp])
            tc = P.add("dve", lambda e: e.tensor_tensor(
                s01r, s01r, rbc[0:1, 0:512], ALU.mult), deps=[tb, te])
            t_pb67 = tb
            tm8 = emit_m8(ctx, s01r, m8r, rscr,
                          [tc, t_routing[-1] if t_routing else None, t_m8rd])
            tbc = P.add("pe", lambda e: e.matmul(
                PB[6][0:8, 0:512], sc["ones_f"][0:1, 0:8], m8r,
                start=True, stop=True), deps=[tm8, tb])
            t_m8rd = tbc
            t1 = P.add("dve", lambda e: e.tensor_scalar(
                wsc[0:8, 0:512], PB[6][0:8, 0:512], sc["iota8c"][0:8, :], None,
                ALU.subtract), deps=[tbc, t_routing[-1] if t_routing else None])
            t2 = P.add("act", lambda e: e.activation(
                wsc[0:8, 0:512], wsc[0:8, 0:512], AF.Abs), deps=[t1])
            t3 = P.add("act", lambda e: e.activation(
                wsc[0:8, 0:512], wsc[0:8, 0:512], AF.Relu, scale=-1.0,
                bias=sc["cvals"][0:8, 2:3]), deps=[t2])
            t4 = P.add("dve", lambda e: e.tensor_scalar(
                wsc2[0:8, 0:512], PB[6][0:8, 0:512], sc["c8c"][0:8, :], None,
                ALU.subtract), deps=[tbc, t3])
            t5 = P.add("act", lambda e: e.activation(
                wsc2[0:8, 0:512], wsc2[0:8, 0:512], AF.Abs), deps=[t4])
            t6 = P.add("act", lambda e: e.activation(
                wsc2[0:8, 0:512], wsc2[0:8, 0:512], AF.Relu, scale=-1.0,
                bias=sc["cvals"][0:8, 2:3]), deps=[t5])
            t7 = P.add("dve", lambda e: e.tensor_tensor(
                wsc[0:8, 0:512], wsc[0:8, 0:512], wsc2[0:8, 0:512], ALU.add),
                deps=[t6, t3])
            t8 = P.add("dve", lambda e, t4_=t4i: e.tensor_scalar(
                wm8[:, t4_ * 512:(t4_ + 1) * 512], wsc[0:8, 0:512], 0.5, None,
                ALU.mult), deps=[t7])
            t_pb67 = t4
            t_routing.append(t8)
            for fc in range(FC):
                tA = P.add("dve", lambda e, fc=fc, tsl=tsl: e.tensor_tensor(
                    asc[0], x_half[:, fc, tsl], mbc, ALU.subtract),
                    deps=[tmb, t_lnapply[0]])
                tB = P.add("dve", lambda e, fc=fc, tsl=tsl: e.tensor_tensor(
                    ln2[:, fc, tsl], asc[0], rbc, ALU.mult), deps=[tA, te])
                t_lnapply[0] = tB
                t_lnapply[1] = tB
        t_ln_done = P.add("dve", lambda e: e.engine_nop(),
                          deps=t_lnapply + t_routing)
        t_wm8_done = t_ln_done
        dbg = ctx.get("dbg") or {}
        if dbg:
            P.dma("sp", lambda e, hsl=hsl: e.dma_start(
                out=dbg["ln2"].rearrange("(f p) t -> p f t", p=128)[:, :, hsl],
                in_=ln2), deps=[t_ln_done])
            P.dma("sp", lambda e, hsl=hsl: e.dma_start(
                out=dbg["wm8"][0:8, hsl], in_=wm8), deps=[t_wm8_done])
            t_dbgx = P.dma("sp", lambda e, hsl=hsl: e.dma_start(
                out=dbg["xpre"].rearrange("(f p) t -> p f t", p=128)[:, :, hsl],
                in_=x_half), deps=[t_ln_done])

        t_pb6 = t_pb67
        tw_load = {}
        tw_free = {}

        def load_piece(eq):
            ei, q = divmod(eq, 4)
            bi = eq % 2
            dep = [tw_free.get(eq - 2), tb1 if eq == 0 else None]
            t1 = P.dma("pool", lambda e, ei=ei, q=q, bi=bi: e.dma_start(
                out=w1b[bi], in_=fW1[ei].rearrange("(f p) d -> p f d", p=128)
                [:, :, q * 1024:(q + 1) * 1024]), deps=dep)
            t2 = P.dma("pool", lambda e, ei=ei, q=q, bi=bi: e.dma_start(
                out=w2b[bi], in_=fW2[ei][q * 1024:(q + 1) * 1024, :]
                .rearrange("(d p) o -> p d o", p=128)), deps=[t1])
            tw_load[eq] = (t1, t2)

        load_piece(0)
        load_piece(1)
        NT5 = LH // 512
        t_wbc_wr = [None] * NT5
        t_wbc_rd = [None] * NT5
        t_pair_free = {0: (dbg and t_dbgx or None), 1: (dbg and t_dbgx or None)}
        HB = [0, 1, 7]
        t_hid_rd = {0: t_pb67, 1: t_pb67, 7: t_pb67}
        t_hsc_rd = [None, None, None]
        t_hsb_mm2 = [None] * 8
        t_xadd_last = dbg and t_dbgx or None

        def emit_wbc(ei):
            nonlocal t_pb6
            for t in range(NT5):
                tp = P.add("pe", lambda e, t=t, ei=ei: e.matmul(
                    PB[6][:, 0:512], sc["sel8"][:, ei * 128:(ei + 1) * 128],
                    wm8[:, t * 512:(t + 1) * 512], start=True, stop=True),
                    deps=[t_wm8_done, t_wbc_wr[t], t_pb6])
                t_wbc_wr[t] = P.add("dve", lambda e, t=t: e.tensor_copy(
                    wbc[:, t * 512:(t + 1) * 512], PB[6][:, 0:512]),
                    deps=[tp, t_wbc_rd[t]])
                t_pb6 = t_wbc_wr[t]

        for ei in range(E):
            emit_wbc(ei)
            for q in range(4):
                eq = ei * 4 + q
                tw1, tw2 = tw_load[eq]
                for t in range(NT5):
                    tsl = slice(t * 512, (t + 1) * 512)
                    t_e2 = [None] * 8
                    tq_last = {}

                    def qmms(xq, dc):
                        pair = xq % 2
                        bank0 = 2 + 2 * pair
                        fns = []
                        for j in range(2):
                            oc = 2 * xq + j
                            pbank = PB[bank0 + j]
                            fns.append(lambda e, pbank=pbank, oc=oc, dc=dc,
                                       bi=eq % 2: e.matmul(
                                pbank[:, 0:512],
                                w2b[bi][:, dc, oc * 128:(oc + 1) * 128],
                                hsb[dc], start=(dc == 0), stop=(dc == 7)))
                        extra = None
                        if ei == 0 and q == 0 and dc == 0:
                            for j in range(2):
                                oc = 2 * xq + j
                                pbank = PB[bank0 + j]
                                fns.append(lambda e, pbank=pbank, oc=oc, t=t:
                                           e.matmul(pbank[:, 0:512],
                                            b2pk[0:8, oc * 128:(oc + 1) * 128],
                                            wm8[:, t * 512:(t + 1) * 512],
                                            start=False, stop=False))
                            extra = ctx["t_b2pk"]
                        tm = P.add("pe", fns, deps=[
                            tw2, t_e2[dc],
                            t_pair_free[pair] if dc == 0 else None,
                            t_wm8_done, extra, tq_last.get(xq)])
                        tq_last[xq] = tm
                        t_hsb_mm2[dc] = tm
                        return tm

                    def qadds(xq):
                        pair = xq % 2
                        bank0 = 2 + 2 * pair
                        prev = [tq_last[xq], t_lnapply[0], t_lnapply[1],
                                t_xadd_last]
                        for j in range(2):
                            oc = 2 * xq + j
                            ta2 = P.add("dve", lambda e, oc=oc, j=j, tsl=tsl,
                                        bank0=bank0: e.tensor_tensor(
                                x_half[:, oc, tsl], x_half[:, oc, tsl],
                                PB[bank0 + j][:, 0:512], ALU.add), deps=prev)
                            prev = [ta2]
                        t_pair_free[pair] = prev[0]
                        return prev[0]

                    for dc in range(8):
                        b3 = dc % 3
                        hb = HB[b3]
                        def mk1(fc, dc=dc, bi=eq % 2, hb=hb, tsl=tsl):
                            return lambda e: e.matmul(
                                PB[hb][:, 0:512],
                                w1b[bi][:, fc, dc * 128:(dc + 1) * 128],
                                ln2[:, fc, tsl],
                                start=(fc == 0), stop=(fc == FC - 1))
                        tm1 = P.add("pe", [mk1(fc) for fc in range(FC)],
                                    deps=[tw1, t_ln_done, t_hid_rd[hb]])
                        bcol = b1sb[:, ei * 32 + q * 8 + dc:
                                    ei * 32 + q * 8 + dc + 1]
                        te1 = P.add("act", lambda e, hb=hb, b3=b3, bcol=bcol:
                                    e.activation(hsc[b3], PB[hb][:, 0:512],
                                                 AF.Relu, bias=bcol),
                                    deps=[tm1, t_hsc_rd[b3]])
                        t_hid_rd[hb] = te1
                        te2 = P.add("dve", lambda e, b3=b3, dc=dc, t=t:
                                    e.tensor_tensor(
                            hsb[dc], hsc[b3], wbc[:, t * 512:(t + 1) * 512],
                            ALU.mult), deps=[te1, t_wbc_wr[t], t_hsb_mm2[dc]])
                        t_hsc_rd[b3] = te2
                        t_wbc_rd[t] = te2
                        t_e2[dc] = te2
                        if dc >= 2:
                            qmms(0, dc - 2)
                    qmms(0, 6)
                    qmms(0, 7)
                    for dc in range(8):
                        qmms(1, dc)
                    t_xadd_last = qadds(0)
                    for dc in range(8):
                        qmms(2, dc)
                    t_xadd_last = qadds(1)
                    for dc in range(8):
                        qmms(3, dc)
                    t_xadd_last = qadds(2)
                    t_xadd_last = qadds(3)
                tw_free[eq] = t_xadd_last
                if eq + 2 < E * 4:
                    load_piece(eq + 2)
        t_out_free = [t_xadd_last]
        tst = P.dma("sp", lambda e, hsl=hsl: e.dma_start(
            out=xsr[:, :, hsl], in_=x_half), deps=t_out_free)
        P.barrier()
    AR.pop()


def emit_attna(ctx):
    P, AR, sc, PB = ctx["P"], ctx["AR"], ctx["sc"], ctx["PB"]
    L, NT512 = ctx["L"], ctx["NT512"]
    xs, aW, ln1s, ktvs = ctx["xs"], ctx["aW"], ctx["ln1s"], ctx["ktvs"]
    xsr = xs.rearrange("(f p) t -> p f t", p=128)
    ln1r = ln1s.rearrange("(f p) t -> p f t", p=128)
    NT128 = L // 128

    AR.push()
    wa = AR.alloc(FC * 3 * D, BF16).rearrange("p (f c) -> p f c", f=FC)
    kvb = AR.alloc(FC * D, BF16).rearrange("p (d h) -> p d h", d=FC)
    ksum = AR.alloc(8, BF16)
    tw = P.dma("pool", lambda e: e.dma_start(
        out=wa, in_=aW.rearrange("(f p) c -> p f c", p=128)))

    AR.push()
    xin = [AR.alloc(FC * 512).rearrange("p (f t) -> p f t", f=FC) for _ in range(2)]
    ln1t = [AR.alloc(FC * 512, BF16).rearrange("p (f t) -> p f t", f=FC)
            for _ in range(2)]
    ktvt = [AR.alloc(4 * 2 * D, BF16).rearrange("p (s c) -> p s c", s=4)
            for _ in range(2)]
    scr = [AR.alloc(512) for _ in range(4)]
    scrE = [AR.alloc(512) for _ in range(2)]
    mbc = AR.alloc(512)
    rbc = AR.alloc(512)
    t_xin_rd = [None, None]
    t_ln_st = [None, None]
    t_ktv_st = [None, None]
    t_pb01 = None
    t_kv_ps = {}
    t_kv_evac = {}
    tln = [None, None]
    for t in range(NT512):
        bi = t % 2
        tsl = slice(t * 512, (t + 1) * 512)
        tl = P.dma("sp", lambda e, bi=bi, tsl=tsl: e.dma_start(
            out=xin[bi], in_=xsr[:, :, tsl]), deps=[t_xin_rd[bi]])
        te, tmb, tsr = norm_stats(
            ctx, lambda fc, bi=bi: xin[bi][:, fc, :], PB[0][:, 0:512],
            PB[1][:, 0:512], scr[0], scr[1], mbc, rbc, [tl, t_pb01],
            1e-5, mean=True)
        t_pb01 = tsr
        for fc in range(FC):
            p = fc % 2
            tA = P.add("dve", lambda e, fc=fc, bi=bi, p=p: e.tensor_tensor(
                scr[2 + p], xin[bi][:, fc, :], mbc, ALU.subtract),
                deps=[tmb, tln[p]])
            tB = P.add("dve", lambda e, fc=fc, bi=bi, p=p: e.tensor_tensor(
                ln1t[bi][:, fc, :], scr[2 + p], rbc, ALU.mult),
                deps=[tA, te, t_ln_st[bi]])
            tln[p] = tB
        t_xin_rd[bi] = tln[0]
        t_ln_st[bi] = P.dma("sp", lambda e, bi=bi, tsl=tsl: e.dma_start(
            out=ln1r[:, :, tsl], in_=ln1t[bi]), deps=[tln[0], tln[1]])
        ev_last = None
        for s in range(4):
            mm_ts = []
            for n in range(4):
                bank = PB[2 + n]
                def mkkv(fc, s=s, n=n, bi=bi, bank=bank):
                    return lambda e: e.matmul(
                        bank[:, 0:512], ln1t[bi][:, fc, s * 128:(s + 1) * 128],
                        wa[:, fc, D + n * 512:D + (n + 1) * 512],
                        start=(fc == 0), stop=(fc == FC - 1))
                mm_ts.append(P.add("pe", [mkkv(fc) for fc in range(FC)],
                                   deps=[tw, tln[0], tln[1], t_kv_ps.get(2 + n)]))
            for n in range(2):
                tk = elu1(ctx, PB[2 + n][:, 0:512],
                          ktvt[bi][:, s, n * 512:(n + 1) * 512],
                          scrE[0], scrE[1], mm_ts[n],
                          extra=[t_ktv_st[bi], ev_last])
                t_kv_ps[2 + n] = tk
                ev_last = tk
            for n in range(2, 4):
                tv = P.add("dve", lambda e, n=n, s=s, bi=bi: e.tensor_copy(
                    ktvt[bi][:, s, n * 512:(n + 1) * 512], PB[2 + n][:, 0:512]),
                    deps=[mm_ts[n], t_ktv_st[bi]])
                t_kv_ps[2 + n] = tv
                ev_last = tv
        t_ktv_st[bi] = P.dma("sp", lambda e, bi=bi, t=t: e.dma_start(
            out=ktvs[t * 512:(t + 1) * 512, :].rearrange("(s p) c -> p s c", p=128),
            in_=ktvt[bi]), deps=[t_kv_ps[2], t_kv_ps[3], t_kv_ps[4], t_kv_ps[5]])
    tA_done = P.barrier()
    AR.pop()

    AR.push()
    ktvt2 = [AR.alloc(2 * D, BF16) for _ in range(2)]
    passes = [(0, 3, [0, 1]), (3, 6, [2, 3]), (6, 8, [4, 5, 6, 7])]
    t_ld_rd = [None, None]
    t_ks_evac = {}
    t_kv_evac = dict(t_kv_evac)
    for (d0, d1, ks_dcs) in passes:
        nkv = d1 - d0
        ks_banks = list(range(2 * nkv, 2 * nkv + len(ks_dcs)))
        mm_last = {}
        ks_last = {}
        for lt in range(NT128):
            bi = lt % 2
            tl = P.dma("sp", lambda e, bi=bi, lt=lt: e.dma_start(
                out=ktvt2[bi], in_=ktvs[lt * 128:(lt + 1) * 128, :]),
                deps=[t_ld_rd[bi]])
            last_pe = None
            for j, dc in enumerate(range(d0, d1)):
                for nh in range(2):
                    bank = PB[2 * j + nh]
                    tmm = P.add("pe", lambda e, dc=dc, nh=nh, bi=bi, lt=lt,
                                bank=bank: e.matmul(
                        bank[:, 0:512], ktvt2[bi][:, dc * 128:(dc + 1) * 128],
                        ktvt2[bi][:, D + nh * 512:D + (nh + 1) * 512],
                        start=(lt == 0), stop=(lt == NT128 - 1)),
                        deps=[tl, mm_last.get((j, nh)),
                              t_kv_evac.get(2 * j + nh) if lt == 0 else None])
                    mm_last[(j, nh)] = tmm
                    last_pe = tmm
            for j, dc in enumerate(ks_dcs):
                bk = ks_banks[j]
                tks = P.add("pe", lambda e, dc=dc, bi=bi, lt=lt, bk=bk: e.matmul(
                    PB[bk][:, 0:1], ktvt2[bi][:, dc * 128:(dc + 1) * 128],
                    sc["ones_b"][:, 0:1],
                    start=(lt == 0), stop=(lt == NT128 - 1)),
                    deps=[tl, ks_last.get(j),
                          t_kv_evac.get(bk) if lt == 0 else None,
                          t_ks_evac.get(bk) if lt == 0 else None])
                ks_last[j] = tks
                last_pe = tks
            t_ld_rd[bi] = last_pe
        for j, dc in enumerate(range(d0, d1)):
            for nh in range(2):
                tkv = P.add("dve", lambda e, dc=dc, nh=nh, j=j: e.tensor_copy(
                    kvb[:, dc, nh * 512:(nh + 1) * 512], PB[2 * j + nh][:, 0:512]),
                    deps=[mm_last[(j, nh)]])
                t_kv_evac[2 * j + nh] = tkv
        for j, dc in enumerate(ks_dcs):
            bk = ks_banks[j]
            tks = P.add("dve", lambda e, dc=dc, bk=bk: e.tensor_copy(
                ksum[:, dc:dc + 1], PB[bk][:, 0:1]), deps=[ks_last[j]])
            t_ks_evac[bk] = tks
    tA2 = P.barrier()
    AR.pop()

    AR.push()
    ln1t2 = [AR.alloc(FC * 512, BF16).rearrange("p (f t) -> p f t", f=FC)
             for _ in range(2)]
    xin2 = [AR.alloc(FC * 512).rearrange("p (f t) -> p f t", f=FC)
            for _ in range(2)]
    qbf = AR.alloc(FC * 512, BF16).rearrange("p (d t) -> p d t", d=FC)
    rbc2 = AR.alloc(512)
    rrow = AR.alloc(512)
    scr2 = [AR.alloc(512) for _ in range(3)]
    t_ln_rd = [None, None]
    t_x_rd = [None, None]
    t_q_ps = [None, None]
    t_q_rd = None
    t_den_ps = None
    t_rbc_ps = None
    t_v_ps = [None, None]
    t_x_st = [None, None]
    for t in range(NT512):
        bi = t % 2
        tsl = slice(t * 512, (t + 1) * 512)
        tl = P.dma("sp", lambda e, bi=bi, tsl=tsl: e.dma_start(
            out=ln1t2[bi], in_=ln1r[:, :, tsl]), deps=[t_ln_rd[bi]])
        tx = P.dma("sp", lambda e, bi=bi, tsl=tsl: e.dma_start(
            out=xin2[bi], in_=xsr[:, :, tsl]), deps=[t_x_rd[bi], t_x_st[bi]])
        t_q = []
        for qc in range(FC):
            hb = qc % 2
            tm = P.add("pe", [
                (lambda e, fc=fc, qc=qc, bi=bi, hb=hb: e.matmul(
                    PB[hb][:, 0:512], wa[:, fc, qc * 128:(qc + 1) * 128],
                    ln1t2[bi][:, fc, :], start=(fc == 0), stop=(fc == FC - 1)))
                for fc in range(FC)],
                deps=[tw, tl, t_q_ps[hb]])
            tq = elu1(ctx, PB[hb][:, 0:512], qbf[:, qc, :], scr2[0], scr2[1],
                      tm, extra=[t_q_rd])
            t_q_ps[hb] = tq
            t_q.append(tq)
        tden = P.add("pe", [
            (lambda e, dc=dc: e.matmul(
                PB[6][0:1, 0:512], ksum[:, dc:dc + 1], qbf[:, dc, :],
                start=(dc == 0), stop=(dc == FC - 1)))
            for dc in range(FC)],
            deps=t_q + [t_den_ps, tA2["pe"]])
        tr1 = P.add("dve", lambda e: e.tensor_scalar(
            rrow[0:1, 0:512], PB[6][0:1, 0:512], 1e-6, None, ALU.add),
            deps=[tden])
        tr2 = P.add("dve", lambda e: e.reciprocal(
            rrow[0:1, 0:512], rrow[0:1, 0:512]), deps=[tr1])
        t_den_ps = tr1
        trb = P.add("pe", lambda e: e.matmul(
            PB[7][:, 0:512], sc["ones_f"][0:1, :], rrow[0:1, 0:512],
            start=True, stop=True), deps=[tr2, t_rbc_ps])
        trc = P.add("dve", lambda e: e.tensor_copy(rbc2, PB[7][:, 0:512]),
                    deps=[trb])
        t_rbc_ps = trc
        t_upd = None
        t_lastvmm = None
        for hc in range(FC):
            vb = 2 + hc % 2
            tm = P.add("pe", [
                (lambda e, dc=dc, hc=hc, vb=vb: e.matmul(
                    PB[vb][:, 0:512], kvb[:, dc, hc * 128:(hc + 1) * 128],
                    qbf[:, dc, :], start=(dc == 0), stop=(dc == FC - 1)))
                for dc in range(FC)],
                deps=t_q + [t_v_ps[vb - 2]])
            t_lastvmm = tm
            tv1 = P.add("dve", lambda e, vb=vb: e.tensor_tensor(
                scr2[2], PB[vb][:, 0:512], rbc2, ALU.mult),
                deps=[tm, trc, t_upd])
            t_upd = P.add("dve", lambda e, hc=hc, bi=bi: e.tensor_tensor(
                xin2[bi][:, hc, :], xin2[bi][:, hc, :], scr2[2], ALU.add),
                deps=[tv1, tx])
            t_v_ps[vb - 2] = tv1
        t_q_rd = t_lastvmm
        t_rbc_ps = t_upd
        t_x_rd[bi] = t_upd
        t_x_st[bi] = P.dma("sp", lambda e, bi=bi, tsl=tsl: e.dma_start(
            out=xsr[:, :, tsl], in_=xin2[bi]), deps=[t_upd])
        t_ln_rd[bi] = t_q[-1]
    AR.pop()
    AR.pop()


def emit_attnb(ctx):
    P, AR, sc, PB = ctx["P"], ctx["AR"], ctx["sc"], ctx["PB"]
    L, NT512 = ctx["L"], ctx["NT512"]
    xs, w_qkv, w_out = ctx["xs"], ctx["w_qkv"], ctx["w_out"]
    g1c, g2c, h2s = ctx["g1c"], ctx["g2c"], ctx["h2s"]
    xsr = xs.rearrange("(f p) t -> p f t", p=128)
    h2r = h2s.rearrange("(f p) t -> p f t", p=128)
    xmean = ctx["xmean"]
    cin = ctx["cin"]

    AR.push()
    wqk = AR.alloc(FC * 3 * D, BF16).rearrange("p (f c) -> p f c", f=FC)
    wo = AR.alloc(FC * D, BF16).rearrange("p (r c) -> p r c", r=FC)
    g1s = AR.alloc(FC)
    g2s = AR.alloc(FC)
    S_sb = AR.alloc(H * HD)
    Z_sb = AR.alloc(H)
    S_bf = AR.alloc(H * HD, BF16)
    Z_bf = AR.alloc(H, BF16)
    x2 = AR.alloc(FC * 512).rearrange("p (f t) -> p f t", f=FC)
    ln3 = AR.alloc(FC * 512).rearrange("p (f t) -> p f t", f=FC)
    tfull = ln3
    h1b = AR.alloc(FC * 512, BF16).rearrange("p (f t) -> p f t", f=FC)
    h2b = h1b
    qpad = AR.alloc(H * 512, BF16).rearrange("p (h t) -> p h t", h=H)
    kpad = AR.alloc(H * 512, BF16).rearrange("p (h t) -> p h t", h=H)
    vtm = AR.alloc(4 * D, BF16).rearrange("p (s c) -> p s c", s=4)
    attn = AR.alloc(FC * 512, BF16).rearrange("p (c t) -> p c t", c=FC)
    qeb = [AR.alloc(512, BF16) for _ in range(3)]
    A_bf = AR.alloc(512, BF16)
    atmp = AR.alloc(256, BF16)
    ktm = [AR.alloc(128, BF16) for _ in range(2)]
    rp = [AR.alloc(512) for _ in range(2)]
    scrE = [AR.alloc(512) for _ in range(2)]
    ascr = [AR.alloc(512) for _ in range(2)]
    sscr = [AR.alloc(512) for _ in range(2)]
    mbc = AR.alloc(512)
    rbc = AR.alloc(512)
    rmsb = AR.alloc(512)
    rrow = AR.alloc(256)
    rbcs = AR.alloc(256)
    xmp = AR.alloc(1)
    cosb = [AR.alloc(512, BF16) for _ in range(2)]
    sinb = [AR.alloc(512, BF16) for _ in range(2)]
    rbcs2dbg = AR.alloc(256)
    abfdbg = AR.alloc(512, BF16)
    dnsc = AR.alloc(256)
    pb0dbg = AR.alloc(512)

    tw1 = P.dma("pool", lambda e: e.dma_start(
        out=wqk, in_=w_qkv.rearrange("(f p) c -> p f c", p=128)))
    tw2 = P.dma("pool", lambda e: e.dma_start(
        out=wo, in_=w_out.rearrange("(f p) c -> p f c", p=128)))
    tg1 = P.dma("sp", lambda e: e.dma_start(out=g1s, in_=g1c[:]))
    tg2 = P.dma("sp", lambda e: e.dma_start(out=g2s, in_=g2c[:]))
    tz0 = P.add("dve", lambda e: e.memset(S_sb, 0.0))
    tz1 = P.add("dve", lambda e: e.memset(Z_sb, 0.0))
    tz2 = P.add("dve", lambda e: e.memset(qpad, 0.0))
    tz3 = P.add("dve", lambda e: e.memset(kpad, 0.0))
    tz4 = P.add("dve", lambda e: e.memset(S_bf, 0.0))
    tz5 = P.add("dve", lambda e: e.memset(Z_bf, 0.0))
    tz6 = P.add("dve", lambda e: e.memset(xmean, 0.0))

    t_x_st = None
    t_pb = {i: None for i in range(8)}
    t_ln3_rd = None
    t_x2_rd = None
    t_h1_rd = None
    t_qp_rd = None
    t_kp_rd = None
    t_vt_rd = None
    t_attn_rd = None
    t_S_rd = tz0
    t_Z_rd = tz1
    t_Sbf_rd = tz4
    t_Zbf_rd = tz5
    t_S_wr = tz0
    t_Z_wr = tz1
    t_qeb_rd = [None, None, None]
    t_Abf_rd = None
    t_atmp_rd = None
    t_ktm_rd = [None, None]
    t_rbcs_rd = None
    t_ktm_rd = [None, None]
    t_rbcs_rd = None
    t_h2_st = None
    t_xm_rd = tz6

    t_cs_rd = [None, None]
    for t in range(NT512):
        bi = t % 2
        tsl = slice(t * 512, (t + 1) * 512)
        tx = P.dma("sp", lambda e, tsl=tsl: e.dma_start(
            out=x2, in_=xsr[:, :, tsl]), deps=[t_x_st, t_x2_rd])
        tcs = P.dma("sp", lambda e, tsl=tsl, bi=bi: e.dma_start(
            out=cosb[bi], in_=cin["cos2"][:, tsl]), deps=[t_cs_rd[bi]])
        tss = P.dma("sp", lambda e, tsl=tsl, bi=bi: e.dma_start(
            out=sinb[bi], in_=cin["sin2"][:, tsl]), deps=[t_cs_rd[bi]])
        te, tmb, tsr = norm_stats(
            ctx, lambda fc: x2[:, fc, :], PB[0][:, 0:512], PB[1][:, 0:512],
            sscr[0], sscr[1], mbc, rbc, [tx, t_pb[0], t_pb[1]], 1e-5, mean=True)
        t_pb[0] = tmb
        t_pb[1] = tsr
        tla = [None, None]
        for fc in range(FC):
            p = fc % 2
            tA = P.add("dve", lambda e, fc=fc, p=p: e.tensor_tensor(
                ascr[p], x2[:, fc, :], mbc, ALU.subtract),
                deps=[tmb, tla[p], t_ln3_rd])
            tB = P.add("dve", lambda e, fc=fc, p=p: e.tensor_tensor(
                ln3[:, fc, :], ascr[p], rbc, ALU.mult), deps=[tA, te])
            tla[p] = tB
        t_ln3 = [tla[0], tla[1]]
        t_xadd = []
        prev = None
        for fc in range(FC):
            prev = P.add("dve", lambda e, fc=fc: e.tensor_tensor(
                x2[:, fc, :], x2[:, fc, :], ln3[:, fc, :], ALU.add),
                deps=t_ln3 + [prev])
            t_xadd.append(prev)
        t_x2_w = t_xadd[-1]
        tre, _, tsr2 = norm_stats(
            ctx, lambda fc: ln3[:, fc, :], None, PB[1][:, 0:512],
            sscr[0], sscr[1], None, rmsb, t_ln3 + [t_pb[1]], 1e-6, mean=False)
        t_pb[1] = tsr2
        th1 = [None, None]
        for fc in range(FC):
            p = fc % 2
            tA = P.add("dve", lambda e, fc=fc, p=p: e.tensor_tensor(
                ascr[p], ln3[:, fc, :], rmsb, ALU.mult),
                deps=[tre, th1[p]])
            tB = P.add("dve", lambda e, fc=fc, p=p: e.tensor_scalar(
                h1b[:, fc, :], ascr[p], g1s[:, fc:fc + 1], None, ALU.mult),
                deps=[tA, tg1, t_h1_rd, t_h2_st])
            th1[p] = tB
        t_h1 = [th1[0], th1[1]]

        t_qp_w = []
        t_kp_w = []
        qk_ps = {}
        t_scrE = None
        t_rp = None

        def qk_fin1(p):
            nonlocal t_scrE, t_rp, t_qeb_rd
            bank, tm = qk_ps[p]
            q3 = p % 3
            tc1 = P.add("dve", lambda e, bank=bank, q3=q3: e.tensor_copy(
                qeb[q3], bank[:, 0:512]), deps=[tm, t_qeb_rd[q3]])
            trot = P.add("pe", lambda e, q3=q3: e.matmul(
                PB[5][:, 0:512], sc["rotP"], qeb[q3], start=True, stop=True),
                deps=[tc1, t_pb[5]])
            tr1 = P.add("dve", lambda e, bank=bank, bi=bi: e.tensor_tensor(
                rp[0], bank[:, 0:512], cosb[bi], ALU.mult),
                deps=[tm, tc1, tcs, t_rp])
            t_pb_set(bank, tr1)
            tr2 = P.add("dve", lambda e, bi=bi: e.tensor_tensor(
                rp[1], PB[5][:, 0:512], sinb[bi], ALU.mult),
                deps=[trot, tr1, tss])
            t_pb[5] = tr2
            tr3 = P.add("dve", lambda e: e.tensor_tensor(
                rp[0], rp[0], rp[1], ALU.add), deps=[tr2])
            tm1 = P.add("dve", lambda e: e.tensor_scalar(
                scrE[0], rp[0], 0.0, None, ALU.min), deps=[tr3, t_scrE])
            tm2 = P.add("act", lambda e: e.activation(scrE[1], scrE[0], AF.Exp),
                        deps=[tm1])
            tm3 = P.add("dve", lambda e, q3=q3: e.scalar_tensor_tensor(
                qeb[q3], rp[0], 0.0, scrE[1], ALU.max, ALU.add),
                deps=[tm2, trot])
            t_scrE = tm3
            t_rp = tm3
            qk_ps[p] = (bank, tm, tm3)

        def qk_fin2(p):
            nonlocal t_qp_rd, t_kp_rd
            bank, tm, tm3 = qk_ps[p]
            q3 = p % 3
            isq = p < 8
            pc = p if isq else p - 8
            pad = qpad if isq else kpad
            tpe = P.add("dve", lambda e, pad=pad, pc=pc, q3=q3: e.tensor_copy(
                pad[0:64, 2 * pc, :], qeb[q3][0:64, :]),
                deps=[tm3, t_qp_rd if isq else t_kp_rd, tz2, tz3])
            tsh = P.add("pe", lambda e, q3=q3: e.matmul(
                PB[7][0:64, 0:512], sc["shift64"], qeb[q3],
                start=True, stop=True), deps=[tm3, t_pb[7]])
            tpo = P.add("dve", lambda e, pad=pad, pc=pc: e.tensor_copy(
                pad[0:64, 2 * pc + 1, :], PB[7][0:64, 0:512]), deps=[tsh, tpe])
            t_pb[7] = tpo
            t_qeb_rd[q3] = tsh
            if isq:
                t_qp_w.append(tpo)
            else:
                t_kp_w.append(tpo)

        def t_pb_set(bank, tick):
            for i in range(8):
                if bank is PB[i]:
                    t_pb[i] = tick
                    return

        for c16 in range(16):
            bank = PB[2 + c16 % 3]
            tm = P.add("pe", [
                (lambda e, fc=fc, c16=c16, bank=bank: e.matmul(
                    bank[:, 0:512], wqk[:, fc, c16 * 128:(c16 + 1) * 128],
                    h1b[:, fc, :], start=(fc == 0), stop=(fc == FC - 1)))
                for fc in range(FC)],
                deps=t_h1 + [tw1, t_pb[2 + c16 % 3]])
            qk_ps[c16] = (bank, tm)
            if c16 >= 1:
                qk_fin1(c16 - 1)
            if c16 >= 2:
                qk_fin2(c16 - 2)
        qk_fin1(15)
        qk_fin2(14)
        qk_fin2(15)
        t_vt_w = []
        for s in range(4):
            for nh in range(2):
                bank = PB[2 + (2 * s + nh) % 2]
                tm = P.add("pe", [
                    (lambda e, fc=fc, s=s, nh=nh, bank=bank: e.matmul(
                        bank[:, 0:512], h1b[:, fc, s * 128:(s + 1) * 128],
                        wqk[:, fc, 2 * D + nh * 512:2 * D + (nh + 1) * 512],
                        start=(fc == 0), stop=(fc == FC - 1)))
                    for fc in range(FC)],
                    deps=t_h1 + [tw1, t_pb[2 + (2 * s + nh) % 2]])
                tv = P.add("dve", lambda e, s=s, nh=nh, bank=bank: e.tensor_copy(
                    vtm[:, s, nh * 512:(nh + 1) * 512], bank[:, 0:512]),
                    deps=[tm, t_vt_rd])
                t_pb[2 + (2 * s + nh) % 2] = tv
                t_vt_w.append(tv)

        t_attn_w = []
        for ci in range(2):
            gc = 2 * t + ci
            csl = slice(ci * 256, (ci + 1) * 256)
            tsb = P.add("dve", lambda e: e.tensor_copy(S_bf[0:64, :], S_sb[0:64, :]),
                        deps=[t_S_wr, t_Sbf_rd])
            tzb = P.add("dve", lambda e: e.tensor_copy(Z_bf[0:64, :], Z_sb[0:64, :]),
                        deps=[t_Z_wr, t_Zbf_rd])
            t_S_rd = tsb
            t_Z_rd = tzb
            Sbf_r = S_bf.rearrange("p (h d) -> p h d", h=H)
            t_s_upd = []
            t_z_upd = []
            for h in range(H):
                pc, half = h // 2, h % 2
                tA1 = P.add("pe", [
                    (lambda e, ph=ph, h=h: e.matmul(
                        PB[0][:, ph * 256:(ph + 1) * 256],
                        kpad[:, h, ci * 256 + ph * 128:ci * 256 + (ph + 1) * 128],
                        qpad[:, h, csl], start=(ph == 0), stop=(ph == 1)))
                    for ph in range(2)],
                    deps=t_kp_w + t_qp_w + [t_pb[0]])
                tmask = P.add("dve", lambda e: e.tensor_tensor(
                    A_bf, PB[0][:, 0:512], sc["tril"], ALU.mult),
                    deps=[tA1, t_Abf_rd])
                t_pb[0] = tmask
                num_fns = [
                    lambda e, h=h: e.matmul(
                        PB[1][0:64, 0:256], Sbf_r[:, h, :], qpad[:, h, csl],
                        start=True, stop=False)]
                for ph in range(2):
                    num_fns.append(lambda e, ph=ph, h=h, ci=ci: e.matmul(
                        PB[1][0:64, 0:256], vtm[:, 2 * ci + ph, h * 64:(h + 1) * 64],
                        A_bf[:, ph * 256:(ph + 1) * 256],
                        start=False, stop=(ph == 1)))
                den_fns = [
                    lambda e, h=h: e.matmul(
                        PB[2][0:1, 0:256], Z_bf[:, h:h + 1], qpad[:, h, csl],
                        start=True, stop=False)]
                for ph in range(2):
                    den_fns.append(lambda e, ph=ph: e.matmul(
                        PB[2][0:1, 0:256], sc["ones_b"][:, 0:1],
                        A_bf[:, ph * 256:(ph + 1) * 256],
                        start=False, stop=(ph == 1)))
                tnd = P.add("pe", num_fns + den_fns,
                            deps=[tmask, tsb, tzb, t_pb[1], t_pb[2]])
                tr1 = P.add("dve", lambda e: e.tensor_scalar(
                    rrow[0:1, 0:256], PB[2][0:1, 0:256], 1e-6, None, ALU.max),
                    deps=[tnd])
                tr2 = P.add("dve", lambda e: e.reciprocal(
                    rrow[0:1, 0:256], rrow[0:1, 0:256]), deps=[tr1])
                trb = P.add("pe", lambda e: e.matmul(
                    PB[2][0:64, 0:256], sc["ones_f"][0:1, 0:64],
                    rrow[0:1, 0:256], start=True, stop=True),
                    deps=[tr2, t_pb[2]])
                trc = P.add("dve", lambda e: e.tensor_copy(
                    rbcs[0:64, 0:256], PB[2][0:64, 0:256]),
                    deps=[trb, t_rbcs_rd])
                t_pb[2] = trc
                if half == 0:
                    tat = P.add("dve", lambda e, pc=pc, csl=csl:
                                e.tensor_tensor(
                        attn[0:64, pc, csl], PB[1][0:64, 0:256],
                        rbcs[0:64, 0:256], ALU.mult),
                        deps=[tnd, trc, t_attn_rd, tr1])
                    t_pb[1] = tat
                    t_rbcs_rd = tat
                else:
                    tat0 = P.add("dve", lambda e: e.tensor_tensor(
                        atmp[0:64, :], PB[1][0:64, 0:256],
                        rbcs[0:64, 0:256], ALU.mult),
                        deps=[tnd, trc, t_atmp_rd, tr1])
                    t_pb[1] = tat0
                    t_rbcs_rd = tat0
                    tsu2 = P.add("pe", lambda e: e.matmul(
                        PB[7][:, 0:256], sc["shiftup"], atmp[0:64, :],
                        start=True, stop=True), deps=[tat0, t_pb[7]])
                    tat = P.add("dve", lambda e, pc=pc, csl=csl: e.tensor_copy(
                        attn[64:128, pc, csl], PB[7][64:128, 0:256]),
                        deps=[tsu2, t_attn_rd])
                    t_pb[7] = tat
                    t_atmp_rd = tsu2
                tsz = []
                for ph in range(2):
                    ttp = P.add("pe", lambda e, ph=ph, h=h, ci=ci: e.transpose(
                        PB[3 + ph][:, 0:256].bitcast(BF16)[:, 0:128],
                        kpad[:, h, ci * 256 + ph * 128:ci * 256 + (ph + 1) * 128],
                        sc["ident_b"]), deps=t_kp_w + [t_pb[3 + ph]])
                    tkc = P.add("dve", lambda e, ph=ph: e.tensor_copy(
                        ktm[ph], PB[3 + ph][:, 0:256].bitcast(BF16)[:, 0:128]),
                        deps=[ttp, t_ktm_rd[ph]])
                    t_pb[3 + ph] = tkc
                    tsz.append(tkc)
                tsu = P.add("pe", [
                    (lambda e, ph=ph, h=h, ci=ci: e.matmul(
                        PB[5][0:64, 0:64], ktm[ph][:, 0:64],
                        vtm[:, 2 * ci + ph, h * 64:(h + 1) * 64],
                        start=(ph == 0), stop=(ph == 1)))
                    for ph in range(2)],
                    deps=tsz + t_vt_w + [t_pb[5]])
                tzu = P.add("pe", [
                    (lambda e, ph=ph: e.matmul(
                        PB[6][0:64, 0:1], ktm[ph][:, 0:64], sc["ones_b"][:, 0:1],
                        start=(ph == 0), stop=(ph == 1)))
                    for ph in range(2)],
                    deps=tsz + [t_pb[6]])
                t_ktm_rd[0] = tzu
                t_ktm_rd[1] = tzu
                tsa = P.add("dve", lambda e, h=h: e.tensor_tensor(
                    S_sb[0:64, h * 64:(h + 1) * 64], S_sb[0:64, h * 64:(h + 1) * 64],
                    PB[5][0:64, 0:64], ALU.add), deps=[tsu, t_S_rd])
                tza = P.add("dve", lambda e, h=h: e.tensor_tensor(
                    Z_sb[0:64, h:h + 1], Z_sb[0:64, h:h + 1], PB[6][0:64, 0:1],
                    ALU.add), deps=[tzu, t_Z_rd])
                t_pb[5] = tsa
                t_pb[6] = tza
                t_s_upd.append(tsa)
                t_z_upd.append(tza)
                t_Abf_rd = tnd
            t_S_wr = t_s_upd[-1]
            t_Z_wr = t_z_upd[-1]
            t_Sbf_rd = tnd
            t_Zbf_rd = tnd
            t_attn_w.append(tat)
        t_qp_rd = tnd
        t_kp_rd = tnd
        t_vt_rd = tnd

        t_tf = []
        t_x2_rd_new = None
        t_wout_last = None
        for oc in range(FC):
            bank = PB[oc % 2]
            tm = P.add("pe", [
                (lambda e, rc=rc, oc=oc, bank=bank: e.matmul(
                    bank[:, 0:512], wo[:, rc, oc * 128:(oc + 1) * 128],
                    attn[:, rc, :], start=(rc == 0), stop=(rc == FC - 1)))
                for rc in range(FC)],
                deps=t_attn_w + [tw2, t_pb[oc % 2]])
            t_wout_last = tm
            ttf = P.add("dve", lambda e, oc=oc, bank=bank: e.tensor_tensor(
                tfull[:, oc, :], ln3[:, oc, :], bank[:, 0:512], ALU.add),
                deps=[tm, t_ln3[oc % 2], t_x2_w])
            txu = P.add("dve", lambda e, oc=oc, bank=bank: e.tensor_tensor(
                x2[:, oc, :], x2[:, oc, :], bank[:, 0:512], ALU.add),
                deps=[ttf])
            t_pb[oc % 2] = txu
            t_tf.append(ttf)
            t_x2_rd_new = txu
        t_ln3_rd = t_tf[-1]
        tre2, _, tsr3 = norm_stats(
            ctx, lambda fc: tfull[:, fc, :], None, PB[1][:, 0:512],
            sscr[0], sscr[1], None, rmsb, t_tf + [t_pb[1]], 1e-6, mean=False)
        t_pb[1] = tsr3
        th2 = [None, None]
        t_xm = t_xm_rd
        for fc in range(FC):
            p = fc % 2
            tA = P.add("dve", lambda e, fc=fc, p=p: e.tensor_tensor(
                ascr[p], tfull[:, fc, :], rmsb, ALU.mult),
                deps=[tre2, th2[p]])
            tB = P.add("dve", lambda e, fc=fc, p=p: e.tensor_scalar(
                h2b[:, fc, :], ascr[p], g2s[:, fc:fc + 1], 0.0, ALU.mult,
                ALU.add, accum_out=xmp[:, 0:1]),
                deps=[tA, tg2, t_h2_st, t_xm, t_h1_rd, tnd])
            t_xm = P.add("dve", lambda e, fc=fc: e.tensor_tensor(
                xmean[:, fc:fc + 1], xmean[:, fc:fc + 1], xmp[:, 0:1], ALU.add),
                deps=[tB])
            th2[p] = t_xm
        t_xm_rd = t_xm
        t_h2_st = P.dma("sp", lambda e, tsl=tsl: e.dma_start(
            out=h2r[:, :, tsl], in_=h2b), deps=[th2[0], th2[1]])
        t_x_st = P.dma("sp", lambda e, tsl=tsl: e.dma_start(
            out=xsr[:, :, tsl], in_=x2), deps=[t_x2_rd_new])
        dbg = ctx.get("dbg") or {}
        if dbg:
            t_attn_rd = P.dma("sp", lambda e, tsl=tsl: e.dma_start(
                out=dbg["attn"].rearrange("(c p) t -> p c t", p=128)[:, :, tsl],
                in_=attn), deps=t_attn_w + [t_wout_last])
            t_h1_dmp = P.dma("sp", lambda e, tsl=tsl: e.dma_start(
                out=dbg["h1"].rearrange("(c p) t -> p c t", p=128)[:, :, tsl],
                in_=h1b), deps=t_h1 + [t_wout_last])
            t_qp_dmp = P.dma("sp", lambda e, t=t: e.dma_start(
                out=dbg["qp"].rearrange("p (h l) -> p h l", h=H)
                [:, :, t * 512:(t + 1) * 512], in_=qpad), deps=[tnd])
            t_kp_dmp = P.dma("sp", lambda e, t=t: e.dma_start(
                out=dbg["kp"].rearrange("p (h l) -> p h l", h=H)
                [:, :, t * 512:(t + 1) * 512], in_=kpad), deps=[tnd])
            t_qp_rd = t_qp_dmp
            t_kp_rd = t_kp_dmp
        t_x2_rd = t_x2_rd_new
        t_h1_rd = tnd if not dbg else t_h1_dmp
        t_attn_rd = t_wout_last if not dbg else t_attn_rd
        t_cs_rd[bi] = tr2
    AR.pop()


def emit_ada(ctx):
    P, AR, sc, PB = ctx["P"], ctx["AR"], ctx["sc"], ctx["PB"]
    L = ctx["L"]
    xs, h2s, hids = ctx["xs"], ctx["h2s"], ctx["hids"]
    w_dp1, dp2c, ghc = ctx["w_dp1"], ctx["dp2c"], ctx["ghc"]
    w_gate, w_up, w_down = ctx["w_gate"], ctx["w_up"], ctx["w_down"]
    cc_in, cc_out = ctx["cc_in"], ctx["cc_out"]
    ncores = ctx["ncores"]
    xmean = ctx["xmean"]
    ccsem = ctx["ccsem"]
    xsr = xs.rearrange("(f p) t -> p f t", p=128)
    h2r = h2s.rearrange("(f p) t -> p f t", p=128)
    hidr = hids.rearrange("(c p) t -> p c t", p=128)
    NT = L // 512

    AR.push()
    wdp1 = AR.alloc(FC * 256, BF16).rearrange("p (f j) -> p f j", f=FC)
    dp2b = AR.alloc(2, BF16)
    xmb = AR.alloc(FC, BF16)
    ybf = AR.alloc(2, BF16)
    srow = AR.alloc(4)
    gath = AR.alloc(128)
    szc = AR.alloc(1)
    rsz = AR.alloc(1)
    maskc = AR.alloc(36)
    rinv = AR.alloc(512)
    td1 = P.dma("pool", lambda e: e.dma_start(
        out=wdp1, in_=w_dp1.rearrange("(f p) j -> p f j", p=128)))
    td2 = P.dma("pool", lambda e: e.dma_start(out=dp2b, in_=dp2c[:]))
    tghs = ctx["tghs"] = None
    txm = P.add("dve", lambda e: e.tensor_scalar(
        xmb, xmean, 1.0 / L, None, ALU.mult))
    tmm1 = P.add("pe", [
        (lambda e, fc=fc, j2=j2: e.matmul(
            PB[0][:, j2:j2 + 1], wdp1[:, fc, j2 * 128:(j2 + 1) * 128],
            xmb[:, fc:fc + 1], start=(fc == 0), stop=(fc == FC - 1)))
        for j2 in range(2) for fc in range(FC)], deps=[td1, txm])
    tsil = P.add("act", lambda e: e.activation(ybf[:, 0:2], PB[0][:, 0:2],
                 AF.Silu), deps=[tmm1])
    tmm2 = P.add("pe", [
        (lambda e, j2=j2: e.matmul(
            PB[1][0:1, 0:1], dp2b[:, j2:j2 + 1], ybf[:, j2:j2 + 1],
            start=(j2 == 0), stop=(j2 == 1)))
        for j2 in range(2)], deps=[td2, tsil])
    tsig = P.add("act", lambda e: e.activation(srow[0:1, 0:1], PB[1][0:1, 0:1],
                 AF.Sigmoid), deps=[tmm2])
    tcl = P.add("dve", lambda e: e.tensor_scalar(
        srow[0:1, 0:1], srow[0:1, 0:1], 0.5, 1.5, ALU.add, ALU.min),
        deps=[tsig])
    tcl2 = P.add("dve", lambda e: e.tensor_scalar(
        srow[0:1, 1:2], srow[0:1, 0:1], 0.5, None, ALU.max), deps=[tcl])
    tz = P.add("dve", lambda e: e.memset(gath[0:1, 0:128], 0.0))
    tcp = P.add("dve", lambda e: e.tensor_copy(gath[0:1, 0:1], srow[0:1, 1:2]),
                deps=[tz, tcl2])
    tci = P.dma("pool", lambda e: e.dma_start(out=cc_in[:], in_=gath[0:1, 0:128]),
                deps=[tcp])
    tcc = P.add("pool", lambda e: e.collective_compute(
        "AllGather", ALU.bypass, replica_groups=[list(range(ncores))],
        ins=[cc_in[:]], outs=[cc_out[:]]).then_inc(ccsem, 1), deps=[tci],
        tick=False)
    tcw = P.add("pool", [lambda e: e.wait_ge(ccsem, 1), lambda e: e.nop()])
    tco = P.dma("pool", lambda e: e.dma_start(
        out=gath[0:ncores, 0:128], in_=cc_out[:]), deps=[tcw])
    tsm = P.add("pe", lambda e: e.matmul(
        PB[2][0:1, 0:1], sc["ones_f"][0:ncores, 0:1], gath[0:ncores, 0:1],
        start=True, stop=True), deps=[tco, tcp])
    ts1 = P.add("dve", lambda e: e.tensor_scalar(
        srow[0:1, 2:3], PB[2][0:1, 0:1], 3072.0 / ncores, None, ALU.mult),
        deps=[tsm])
    ts2 = P.add("dve", lambda e: e.tensor_scalar(
        srow[0:1, 2:3], srow[0:1, 2:3], 0.5, MAGIC, ALU.subtract, ALU.add),
        deps=[ts1])
    ts3 = P.add("dve", lambda e: e.tensor_scalar(
        srow[0:1, 2:3], srow[0:1, 2:3], MAGIC, 1.0, ALU.subtract, ALU.max),
        deps=[ts2])
    trz = P.add("dve", lambda e: e.reciprocal(rsz[0:1, 0:1], srow[0:1, 2:3]),
                deps=[ts3])
    tsb = P.add("pe", lambda e: e.matmul(
        PB[3][:, 0:1], sc["ones_f"][0:1, :], srow[0:1, 2:3],
        start=True, stop=True), deps=[ts3, tcl2])
    tsc = P.add("dve", lambda e: e.tensor_copy(szc, PB[3][:, 0:1]), deps=[tsb])
    tmk = P.add("dve", lambda e: e.tensor_scalar(
        maskc, sc["iota36"], szc[:, 0:1], -1.0, ALU.subtract, ALU.mult),
        deps=[tsc])
    tmk2 = P.add("dve", lambda e: e.tensor_scalar(
        maskc, maskc, 1.0, 0.0, ALU.min, ALU.max), deps=[tmk])

    rbst = AR.alloc(NT * 512, BF16).rearrange("p (t c) -> p t c", t=NT)
    AR.push()
    wg = AR.alloc(FC * FADA, BF16).rearrange("p (f c) -> p f c", f=FC)
    wu = AR.alloc(FC * FADA, BF16).rearrange("p (f c) -> p f c", f=FC)
    h2t = [AR.alloc(FC * 512, BF16).rearrange("p (f t) -> p f t", f=FC)
           for _ in range(2)]
    gsb = [AR.alloc(512, BF16) for _ in range(2)]
    hvs = [AR.alloc(512) for _ in range(2)]
    hbt = [AR.alloc(512, BF16) for _ in range(2)]
    sqb = [AR.alloc(512, BF16) for _ in range(3)]
    rrow2 = AR.alloc(512)
    twg = P.dma("pool", lambda e: e.dma_start(
        out=wg, in_=w_gate.rearrange("(f p) c -> p f c", p=128)))
    twu = P.dma("pool", lambda e: e.dma_start(
        out=wu, in_=w_up.rearrange("(f p) c -> p f c", p=128)))
    t_h2_rd = [None, None]
    t_pb = {i: None for i in range(8)}
    t_pb[0] = tsil
    t_pb[1] = tsig
    t_pb[2] = ts1
    t_pb[3] = tsc
    t_gsb = [None, None]
    t_hvs = [None, None]
    t_hbt = [None, None]
    t_hbt_sq = [None, None]
    t_sqb = [None, None, None]
    t_rr = None
    t_rb_rd = None
    for t in range(NT):
        bi = t % 2
        tsl = slice(t * 512, (t + 1) * 512)
        tl = P.dma("sp", lambda e, bi=bi, tsl=tsl: e.dma_start(
            out=h2t[bi], in_=h2r[:, :, tsl]), deps=[t_h2_rd[bi]])
        t_sq_last = None
        t_sqq = []
        for dc in range(36):
            gb = dc % 2
            ub = 2 + dc % 2
            tg = P.add("pe", [
                (lambda e, fc=fc, dc=dc, bi=bi, gb=gb: e.matmul(
                    PB[gb][:, 0:512], wg[:, fc, dc * 128:(dc + 1) * 128],
                    h2t[bi][:, fc, :], start=(fc == 0), stop=(fc == FC - 1)))
                for fc in range(FC)], deps=[twg, tl, t_pb[gb]])
            tu = P.add("pe", [
                (lambda e, fc=fc, dc=dc, bi=bi, ub=ub: e.matmul(
                    PB[ub][:, 0:512], wu[:, fc, dc * 128:(dc + 1) * 128],
                    h2t[bi][:, fc, :], start=(fc == 0), stop=(fc == FC - 1)))
                for fc in range(FC)], deps=[twu, tl, t_pb[ub]])
            tsi = P.add("act", lambda e, gb=gb: e.activation(
                gsb[gb], PB[gb][:, 0:512], AF.Silu), deps=[tg, t_gsb[gb]])
            t_pb[gb] = tsi
            thv = P.add("dve", lambda e, gb=gb, ub=ub: e.tensor_tensor(
                hvs[gb], PB[ub][:, 0:512], gsb[gb], ALU.mult),
                deps=[tu, tsi, t_hvs[gb]])
            t_gsb[gb] = thv
            t_pb[ub] = thv
            thb = P.add("dve", lambda e, gb=gb, dc=dc: e.tensor_scalar(
                hbt[gb], hvs[gb], maskc[:, dc:dc + 1], None, ALU.mult),
                deps=[thv, tmk2, t_hbt[gb], t_hbt_sq[gb]])
            t_hvs[gb] = thb
            tst = P.dma("sp", lambda e, gb=gb, dc=dc, tsl=tsl: e.dma_start(
                out=hidr[:, dc, tsl], in_=hbt[gb]), deps=[thb])
            t_hbt[gb] = tst
            s3 = dc % 3
            tsq = P.add("act", lambda e, gb=gb, s3=s3: e.activation(
                sqb[s3], hbt[gb], AF.Square), deps=[thb, t_sqb[s3]])
            t_hbt_sq[gb] = tsq
            t_sqq.append((dc, s3, tsq))
            if dc >= 2:
                d2, s32, tsq2 = t_sqq.pop(0)
                t_sq_last = P.add("pe", lambda e, s32=s32, d2=d2: e.matmul(
                    PB[4][0:1, 0:512], sc["ones_b"][:, 0:1], sqb[s32],
                    start=(d2 == 0), stop=(d2 == 35)),
                    deps=[tsq2, t_sq_last, t_rr if d2 == 0 else None])
                t_sqb[s32] = t_sq_last
        while t_sqq:
            d2, s32, tsq2 = t_sqq.pop(0)
            t_sq_last = P.add("pe", lambda e, s32=s32, d2=d2: e.matmul(
                PB[4][0:1, 0:512], sc["ones_b"][:, 0:1], sqb[s32],
                start=(d2 == 0), stop=(d2 == 35)),
                deps=[tsq2, t_sq_last, t_rr if d2 == 0 else None])
            t_sqb[s32] = t_sq_last
        tr1 = P.add("dve", lambda e: e.tensor_scalar(
            rrow2[0:1, 0:512], PB[4][0:1, 0:512], rsz[0:1, 0:1], None,
            ALU.mult), deps=[t_sq_last, trz])
        t_rr = tr1
        tr2 = P.add("act", lambda e: e.activation(
            rrow2[0:1, 0:512], rrow2[0:1, 0:512], AF.Ln,
            bias=sc["cvals"][0:1, 1:2]), deps=[tr1])
        tr3 = P.add("act", lambda e: e.activation(
            rrow2[0:1, 0:512], rrow2[0:1, 0:512], AF.Exp, scale=-0.5),
            deps=[tr2])
        trb = P.add("pe", lambda e: e.matmul(
            PB[5][:, 0:512], sc["ones_f"][0:1, :], rrow2[0:1, 0:512],
            start=True, stop=True), deps=[tr3, t_pb[5]])
        trc = P.add("dve", lambda e, t=t: e.tensor_copy(
            rbst[:, t, :], PB[5][:, 0:512]), deps=[trb, t_rb_rd])
        t_pb[5] = trc
        t_h2_rd[bi] = tu
    P.barrier()
    AR.pop()

    wd = AR.alloc(36 * D, BF16).rearrange("p (c o) -> p c o", c=36)
    ghs = AR.alloc(36)
    x_half = AR.alloc(FC * (L // 2)).rearrange("p (f t) -> p f t", f=FC)
    hidc = [AR.alloc(512, BF16) for _ in range(2)]
    hrb = [AR.alloc(512, BF16) for _ in range(2)]
    twd = P.dma("pool", lambda e: e.dma_start(
        out=wd, in_=w_down.rearrange("(c p) o -> p c o", p=128)))
    tgh = P.dma("sp", lambda e: e.dma_start(out=ghs, in_=ghc[:]))
    tfold = None
    for dc in range(36):
        tfold = P.add("dve", lambda e, dc=dc: e.tensor_scalar(
            wd[:, dc, :], wd[:, dc, :], ghs[:, dc:dc + 1], None, ALU.mult),
            deps=[twd, tgh, tfold])
    t_x_st = None
    t_hid_rd = [None, None]
    t_hrb_rd = [None, None]
    LH = L // 2
    for half in range(2):
        hsl = slice(half * LH, (half + 1) * LH)
        tlx = P.dma("sp", lambda e, hsl=hsl: e.dma_start(
            out=x_half, in_=xsr[:, :, hsl]), deps=[t_x_st])
        t_add_last = None
        for t4 in range(LH // 512):
            gt = half * (LH // 512) + t4
            tsl = slice(t4 * 512, (t4 + 1) * 512)
            t_mm_last = None
            for dc in range(36):
                hb = dc % 2
                tlh = P.dma("sp", lambda e, hb=hb, dc=dc, gt=gt: e.dma_start(
                    out=hidc[hb], in_=hidr[:, dc, gt * 512:(gt + 1) * 512]),
                    deps=[t_hid_rd[hb]])
                thr = P.add("dve", lambda e, hb=hb, gt=gt: e.tensor_tensor(
                    hrb[hb], hidc[hb], rbst[:, gt, :], ALU.mult),
                    deps=[tlh, t_hrb_rd[hb]])
                t_hid_rd[hb] = thr
                tmm = P.add("pe", [
                    (lambda e, oc=oc, dc=dc, hb=hb: e.matmul(
                        PB[oc][:, 0:512], wd[:, dc, oc * 128:(oc + 1) * 128],
                        hrb[hb], start=(dc == 0), stop=(dc == 35)))
                    for oc in range(FC)],
                    deps=[thr, tfold, t_mm_last, t_add_last if dc == 0 else None])
                t_hrb_rd[hb] = tmm
                t_mm_last = tmm
            prev = [t_mm_last]
            for oc in range(FC):
                ta = P.add("dve", lambda e, oc=oc, t4=t4, tsl=tsl: e.tensor_tensor(
                    x_half[:, oc, tsl], x_half[:, oc, tsl], PB[oc][:, 0:512],
                    ALU.add), deps=prev + [tlx])
                prev = [ta]
            t_add_last = prev[0]
        t_x_st = P.dma("sp", lambda e, hsl=hsl: e.dma_start(
            out=xsr[:, :, hsl], in_=x_half), deps=[t_add_last])
    AR.pop()


def build(L, phases, ncores=NCORE):
    nc = bass.Bass("TRN2", target_bir_lowering=False)
    NT512 = L // 512
    LH = L // 2
    NHT = LH // 512

    x_in = nc.dram_tensor("x_in", [L, D], F32, kind="ExternalInput")
    aW = nc.dram_tensor("aW", [D, 3 * D], F32, kind="ExternalInput")
    w_qkv = nc.dram_tensor("w_qkv", [D, 3 * D], F32, kind="ExternalInput")
    w_out = nc.dram_tensor("w_out", [D, D], F32, kind="ExternalInput")
    g1c = nc.dram_tensor("g1c", [128, FC], F32, kind="ExternalInput")
    g2c = nc.dram_tensor("g2c", [128, FC], F32, kind="ExternalInput")
    w_dp1 = nc.dram_tensor("w_dp1", [D, 256], F32, kind="ExternalInput")
    dp2c = nc.dram_tensor("dp2c", [128, 2], F32, kind="ExternalInput")
    w_gate = nc.dram_tensor("w_gate", [D, FADA], F32, kind="ExternalInput")
    w_up = nc.dram_tensor("w_up", [D, FADA], F32, kind="ExternalInput")
    w_down = nc.dram_tensor("w_down", [FADA, D], F32, kind="ExternalInput")
    ghc = nc.dram_tensor("ghc", [128, 36], F32, kind="ExternalInput")
    fW1 = nc.dram_tensor("fW1", [E, D, DFF], F32, kind="ExternalInput")
    b1c = nc.dram_tensor("b1c", [128, E * 32], F32, kind="ExternalInput")
    fW2 = nc.dram_tensor("fW2", [E, DFF, D], F32, kind="ExternalInput")
    fb2 = nc.dram_tensor("fb2", [E, D], F32, kind="ExternalInput")
    y_out = nc.dram_tensor("y_out", [L, D], F32, kind="ExternalOutput")

    CONSTS = host_constants(L)
    cin = {}
    for k, v in CONSTS.items():
        dt = BF16 if v.dtype == ml_dtypes.bfloat16 else F32
        cin[k] = nc.dram_tensor("c_" + k, list(v.shape), dt, kind="ExternalInput")

    xs = nc.dram_tensor("xs", [D, L], F32)
    tsd = nc.dram_tensor("tsd", [D, L], F32)
    ln1s = nc.dram_tensor("ln1s", [D, L], BF16)
    ktvs = nc.dram_tensor("ktvs", [L, 2 * D], BF16)
    h2s = nc.dram_tensor("h2s", [D, L], BF16)
    hids = nc.dram_tensor("hids", [FADA, L], BF16)
    cc_in = nc.dram_tensor("cc_in", [1, 128], F32)
    cc_out = nc.dram_tensor("cc_out", [ncores, 128], F32, addr_space="Shared")
    KDBG = os.environ.get("KDBG", "") == "1"
    dbg = {}
    if KDBG:
        dbg["ln2"] = nc.dram_tensor("dbg_ln2", [D, L], BF16, kind="ExternalOutput")
        dbg["m8"] = nc.dram_tensor("dbg_m8", [1, L], F32, kind="ExternalOutput")
        dbg["wm8"] = nc.dram_tensor("dbg_wm8", [8, L], BF16, kind="ExternalOutput")
        dbg["xpre"] = nc.dram_tensor("dbg_xpre", [D, L], F32, kind="ExternalOutput")
        dbg["attn"] = nc.dram_tensor("dbg_attn", [D, L], BF16, kind="ExternalOutput")
        dbg["h1"] = nc.dram_tensor("dbg_h1", [D, L], BF16, kind="ExternalOutput")
        dbg["qp"] = nc.dram_tensor("dbg_qp", [128, H * L], BF16, kind="ExternalOutput")
        dbg["kp"] = nc.dram_tensor("dbg_kp", [128, H * L], BF16, kind="ExternalOutput")
        dbg["den0"] = nc.dram_tensor("dbg_den0", [1, 256], F32, kind="ExternalOutput")
        dbg["num0"] = nc.dram_tensor("dbg_num0", [64, 256], F32, kind="ExternalOutput")
        dbg["abf0"] = nc.dram_tensor("dbg_abf0", [128, 512], BF16, kind="ExternalOutput")
        dbg["qmm"] = nc.dram_tensor("dbg_qmm", [128, 256], BF16, kind="ExternalOutput")
        dbg["kmm"] = nc.dram_tensor("dbg_kmm", [128, 128], BF16, kind="ExternalOutput")
        dbg["pb0"] = nc.dram_tensor("dbg_pb0", [128, 512], F32, kind="ExternalOutput")

    from contextlib import ExitStack
    es = ExitStack()
    AR = Arena(es.enter_context(nc.sbuf_tensor("AR", [128, 51712], F32)), 51712)
    SM = Arena(es.enter_context(nc.sbuf_tensor("SM", [128, 1280], F32)), 1280)

    sc = {}
    for k, v in CONSTS.items():
        if k in ("cos2", "sin2"):
            continue
        p, n = v.shape
        dtype = BF16 if v.dtype == ml_dtypes.bfloat16 else F32
        ncols_f32 = n if dtype == F32 else (n + 1) // 2
        a = SM if SM.off + ncols_f32 <= SM.cols else AR
        sc[k] = a.alloc(n, dtype)[:p, :]

    b2pk = AR.alloc(D, BF16)[:8, :]
    xmean = AR.alloc(FC)

    PB = [es.enter_context(nc.psum_tensor(f"PB{i}", [128, 512], F32))
          for i in range(8)]

    block = es.enter_context(nc.Block())
    P = Prog(nc)
    P.make_sems()
    ccsem = es.enter_context(nc.semaphore("ccsem"))

    ctx = dict(nc=nc, P=P, AR=AR, SM=SM, sc=sc, PB=PB, L=L, NT512=NT512,
               ncores=ncores, ccsem=ccsem,
               LH=LH, NHT=NHT, NCH=L // CHUNK,
               x_in=x_in, aW=aW, w_qkv=w_qkv, w_out=w_out, g1c=g1c, g2c=g2c,
               w_dp1=w_dp1, dp2c=dp2c, w_gate=w_gate, w_up=w_up,
               w_down=w_down, ghc=ghc, fW1=fW1, b1c=b1c, fW2=fW2, fb2=fb2,
               y_out=y_out, CONSTS=CONSTS, cin=cin, xs=xs, tsd=tsd, dbg=dbg,
               ln1s=ln1s, ktvs=ktvs, h2s=h2s, hids=hids, cc_in=cc_in,
               cc_out=cc_out, b2pk=b2pk, xmean=xmean)

    for k, v in CONSTS.items():
        if k in ("cos2", "sin2"):
            continue
        src, dst = cin[k], sc[k]
        P.dma("sp", lambda e, s=src, d=dst: e.dma_start(out=d, in_=s[:]))
    ctx["t_b2pk"] = P.dma("pool", lambda e: e.dma_start(out=b2pk, in_=fb2[:]))
    P.barrier()

    emit_tin(ctx)
    P.barrier()
    if "attna" in phases:
        emit_attna(ctx)
        P.barrier()
    if "moe1" in phases:
        emit_moe(ctx)
        P.barrier()
    if "attnb" in phases:
        emit_attnb(ctx)
        P.barrier()
    if "ada" in phases:
        emit_ada(ctx)
        P.barrier()
    if "moe2" in phases:
        emit_moe(ctx)
        P.barrier()
    emit_tout(ctx)
    P.barrier()
    P.add("sp", lambda e: e.nop(), deps=P.final_waits(), tick=False)
    P.build(block)
    return nc, es


_CACHE = {}
_LAST_RES = None


def kernel(L0_attn_a_x, L0_attn_a_W, L0_attn_b_w_qkv, L0_attn_b_w_out,
           L0_attn_b_g1, L0_attn_b_g2, L0_attn_b_w_dp1, L0_attn_b_w_dp2,
           L0_attn_b_w_gate, L0_attn_b_w_up, L0_attn_b_w_down,
           L0_attn_b_g_hidden, L0_ffn_W1, L0_ffn_b1, L0_ffn_W2, L0_ffn_b2):
    x = np.ascontiguousarray(np.asarray(L0_attn_a_x, dtype=np.float32))
    B, L, _ = x.shape
    phases = tuple(p for p in os.environ.get(
        "KPHASES", "attna,moe1,attnb,ada,moe2").split(",") if p)
    key = (L, phases, NCORE)
    if key not in _CACHE:
        _CACHE[key] = build(L, phases, NCORE)
    nc, _es = _CACHE[key]

    consts = host_constants(L)
    common = {
        "aW": np.ascontiguousarray(L0_attn_a_W, dtype=np.float32),
        "w_qkv": np.ascontiguousarray(L0_attn_b_w_qkv, dtype=np.float32),
        "w_out": np.ascontiguousarray(L0_attn_b_w_out, dtype=np.float32),
        "g1c": np.ascontiguousarray(np.asarray(L0_attn_b_g1, np.float32).reshape(FC, 128).T),
        "g2c": np.ascontiguousarray(np.asarray(L0_attn_b_g2, np.float32).reshape(FC, 128).T),
        "w_dp1": np.ascontiguousarray(L0_attn_b_w_dp1, dtype=np.float32),
        "dp2c": np.ascontiguousarray(np.asarray(L0_attn_b_w_dp2, np.float32).reshape(2, 128).T),
        "w_gate": np.ascontiguousarray(L0_attn_b_w_gate, dtype=np.float32),
        "w_up": np.ascontiguousarray(L0_attn_b_w_up, dtype=np.float32),
        "w_down": np.ascontiguousarray(L0_attn_b_w_down, dtype=np.float32),
        "ghc": np.ascontiguousarray(np.asarray(L0_attn_b_g_hidden, np.float32).reshape(36, 128).T),
        "fW1": np.ascontiguousarray(L0_ffn_W1, dtype=np.float32),
        "b1c": np.ascontiguousarray(
            np.asarray(L0_ffn_b1, np.float32).reshape(E, 32, 128)
            .transpose(2, 0, 1).reshape(128, E * 32)),
        "fW2": np.ascontiguousarray(L0_ffn_W2, dtype=np.float32),
        "fb2": np.ascontiguousarray(L0_ffn_b2, dtype=np.float32),
    }
    for k, v in consts.items():
        common["c_" + k] = np.ascontiguousarray(v)
    in_maps = []
    for c in range(NCORE):
        m = dict(common)
        m["x_in"] = np.ascontiguousarray(x[c])
        in_maps.append(m)
    res = run_bass_kernel_spmd(nc, in_maps, core_ids=list(range(NCORE)))
    global _LAST_RES
    _LAST_RES = res
    out = np.stack([np.asarray(res.results[c]["y_out"]) for c in range(NCORE)],
                   axis=0)
    return out.astype(np.float32)


# version 09 (final)
# speedup vs baseline: 1.0625x; 1.0625x over previous
import os
import numpy as np
import ml_dtypes
import concourse.bass as bass
import concourse.mybir as mybir
from concourse.bass_utils import run_bass_kernel_spmd

F32 = mybir.dt.float32
BF16 = mybir.dt.bfloat16
AF = mybir.ActivationFunctionType
ALU = mybir.AluOpType
MAGIC = 12582912.0

D = 1024
FC = 8
H = 16
HD = 64
CHUNK = 256
E = 8
DFF = 4096
FADA = 4608
NCORE = 8

ENGINES = ("sp", "pool", "pe", "act", "dve")
_ENG_ATTR = {"sp": "sync", "pool": "gpsimd", "pe": "tensor", "act": "scalar",
             "dve": "vector"}


class Ticket:
    __slots__ = ("engine", "sem", "value", "seq", "is_dma")

    def __init__(self, engine, sem, value, seq, is_dma=False):
        self.engine = engine
        self.sem = sem
        self.value = value
        self.seq = seq
        self.is_dma = is_dma


class Prog:
    def __init__(self, nc):
        self.nc = nc
        self.steps = {e: [] for e in ENGINES}
        self.sems = {}
        self.counts = {}
        self.seq = {e: 0 for e in ENGINES}
        self.hwm = {e: {} for e in ENGINES}
        self._sem_ctxs = []

    NDMA = 8

    def make_sems(self):
        for e in ENGINES:
            ctx = self.nc.semaphore(f"sem_{e}_op")
            self.sems[(e, "op")] = ctx.__enter__()
            self._sem_ctxs.append(ctx)
        self.dma_sems = {}
        self.dma_uses = {}
        self.dma_rot = {}
        for e in ("sp", "pool", "act"):
            self.dma_sems[e] = []
            for i in range(self.NDMA):
                ctx = self.nc.semaphore(f"dma_{e}_{i}")
                self.dma_sems[e].append(ctx.__enter__())
                self._sem_ctxs.append(ctx)
            self.dma_uses[e] = [0] * self.NDMA
            self.dma_rot[e] = 0

    def add(self, engine, fns, deps=(), inc=1, tick=True):
        assert inc == 1
        if callable(fns):
            fns = [fns]
        waits = self._mk_waits(engine, deps)
        sem = self.sems[(engine, "op")]
        ckey = (engine, "op")
        if ckey not in self.counts:
            self.counts[ckey] = 0
        if tick:
            self.counts[ckey] += inc
        value = self.counts[ckey]
        self.seq[engine] += 1
        self.steps[engine].append((waits, fns, sem if tick else None, inc))
        return Ticket(engine, sem, value, self.seq[engine], is_dma=False)

    NO_ELIDE = os.environ.get("KNOELIDE", "") == "1"

    def _mk_waits(self, engine, deps):
        waits = []
        hw = self.hwm[engine]
        best = {}
        for d in deps:
            if d is None:
                continue
            if (not self.NO_ELIDE) and hw.get(id(d.sem), -1) >= d.value:
                continue
            k = id(d.sem)
            if k in best and best[k][1] >= d.value:
                continue
            best[k] = (d.sem, d.value)
            hw[k] = max(hw.get(k, -1), d.value)
        return list(best.values())

    def dma(self, engine, fns, deps=()):
        assert engine in ("sp", "pool", "act")
        if callable(fns):
            fns = [fns]
        k = self.dma_rot[engine]
        self.dma_rot[engine] = (k + 1) % self.NDMA
        sem = self.dma_sems[engine][k]
        prev = self.dma_uses[engine][k] * 16
        waits = self._mk_waits(engine, deps)
        if prev > 0 and self.hwm[engine].get(id(sem), -1) < prev:
            waits.append((sem, prev))
            self.hwm[engine][id(sem)] = prev
        self.dma_uses[engine][k] += 1
        value = self.dma_uses[engine][k] * 16
        self.seq[engine] += 1
        self.steps[engine].append((waits, fns, sem, 16))
        return Ticket(engine, sem, value, self.seq[engine], is_dma=True)

    def now(self):
        out = []
        for (e, kind), cnt in self.counts.items():
            if cnt > 0:
                out.append(Ticket(e, self.sems[(e, kind)], cnt, self.seq[e]))
        for e in self.dma_sems:
            for i, sem in enumerate(self.dma_sems[e]):
                if self.dma_uses[e][i] > 0:
                    out.append(Ticket(e, sem, self.dma_uses[e][i] * 16,
                                      self.seq[e], is_dma=True))
        return out

    def barrier(self):
        cur = self.now()
        outs = {}
        for e in ENGINES:
            outs[e] = self.add(e, lambda eng: eng.nop(), deps=cur, inc=1)
        return outs

    def build(self, block):
        def mk(engine):
            def body(eng):
                for waits, fns, sem, inc in self.steps[engine]:
                    for s, v in waits:
                        eng.wait_ge(s, v)
                    inst = None
                    for f in fns:
                        inst = f(eng)
                    if sem is not None:
                        inst.then_inc(sem, inc)
            return body
        for e in ENGINES:
            if self.steps[e]:
                getattr(block, _ENG_ATTR[e])(mk(e))

    def final_waits(self):
        return self.now()


class Arena:

    def __init__(self, tensor, cols):
        self.t = tensor
        self.cols = cols
        self.off = 0
        self.marks = []

    def alloc(self, n, dtype=F32):
        nf = n if dtype == F32 else (n + 1) // 2
        assert self.off + nf <= self.cols, f"arena overflow {self.off}+{nf}>{self.cols}"
        ap = self.t[:, self.off:self.off + nf]
        self.off += nf
        if dtype != F32:
            ap = ap.bitcast(BF16)
        return ap

    def push(self):
        self.marks.append(self.off)

    def pop(self):
        self.off = self.marks.pop()


def host_constants(L):
    c = {}
    c["ones_f"] = np.ones((128, 128), dtype=np.float32)
    c["ones_b"] = np.ones((128, 128), dtype=ml_dtypes.bfloat16)
    c["ident_f"] = np.eye(128, dtype=np.float32)
    c["ident_b"] = np.eye(128, dtype=ml_dtypes.bfloat16)
    e01 = np.zeros((128, 1), dtype=np.float32)
    e01[0, 0] = 1.0
    e01[1, 0] = 1.0
    c["e01"] = e01
    c["iota8c"] = np.arange(128, dtype=np.float32).reshape(128, 1)
    c["c8c"] = ((np.arange(128) - 1) % 8).astype(np.float32).reshape(128, 1)
    sel = np.zeros((8, 8 * 128), dtype=np.float32)
    for e in range(8):
        sel[e, e * 128:(e + 1) * 128] = 1.0
    c["sel8"] = sel.astype(ml_dtypes.bfloat16)
    pos = np.arange(L, dtype=np.float64)
    inv_freq = 1.0 / (10000.0 ** (np.arange(0, HD, 2, dtype=np.float64) / HD))
    freqs = pos[:, None] * inv_freq[None, :]
    emb = np.concatenate([freqs, freqs], axis=-1)
    cos = np.cos(emb).T.astype(np.float32)
    sin = np.sin(emb).T.astype(np.float32)
    c["cos2"] = np.concatenate([cos, cos], 0).astype(ml_dtypes.bfloat16)
    c["sin2"] = np.concatenate([sin, sin], 0).astype(ml_dtypes.bfloat16)
    P64 = np.zeros((64, 64), dtype=np.float32)
    for i in range(32):
        P64[2 * i + 1, 2 * i] = -1.0
        P64[2 * i, 2 * i + 1] = 1.0
    P128 = np.zeros((128, 128), dtype=np.float32)
    P128[:64, :64] = P64
    P128[64:, 64:] = P64
    c["rotP"] = P128.astype(ml_dtypes.bfloat16)
    SH = np.zeros((128, 64), dtype=np.float32)
    for m in range(64):
        SH[m + 64, m] = 1.0
    c["shift64"] = SH.astype(ml_dtypes.bfloat16)
    SU = np.zeros((64, 128), dtype=np.float32)
    for d in range(64):
        SU[d, d + 64] = 1.0
    c["shiftup"] = SU.astype(ml_dtypes.bfloat16)
    m0 = (np.arange(128)[:, None] <= np.arange(256)[None, :]).astype(np.float32)
    m1 = ((np.arange(128) + 128)[:, None] <= np.arange(256)[None, :]).astype(np.float32)
    c["tril"] = np.concatenate([m0, m1], axis=1).astype(ml_dtypes.bfloat16)
    c["iota36"] = (np.arange(36)[None, :] * 128
                   + np.arange(128)[:, None]).astype(np.float32)
    c["cvals"] = np.repeat(np.array([[1e-5, 1e-6, 1.0]], dtype=np.float32),
                           128, axis=0)
    return c


def norm_stats(ctx, x_fc, pb1, pb2, scr_a, scr_b, out_mbc, out_rbc, deps_in,
               eps, mean=True):
    P = ctx["P"]
    if not isinstance(deps_in, (list, tuple)):
        deps_in = [deps_in]
    deps_in = list(deps_in)
    ones = ctx["sc"]["ones_f"]
    t1 = None
    if mean:
        def mk(fc):
            return lambda e: e.matmul(pb1, ones, x_fc(fc),
                                      start=(fc == 0), stop=(fc == FC - 1))
        t1 = P.add("pe", [mk(fc) for fc in range(FC)], deps=deps_in)
    tsq_prev = [None, None]
    tmm = None
    ones_b = ctx["sc"]["ones_b"]
    for fc in range(FC):
        scr = (scr_a if fc % 2 == 0 else scr_b).bitcast(BF16)[:, 0:512]
        ta = P.add("act", lambda e, s=x_fc(fc), d=scr: e.activation(d, s, AF.Square),
                   deps=deps_in + [tsq_prev[fc % 2]])
        tmm = P.add("pe", lambda e, d=scr, fc=fc: e.matmul(pb2, ones_b, d,
                    start=(fc == 0), stop=(fc == FC - 1)),
                    deps=[ta] + ([t1] if t1 is not None else []))
        tsq_prev[fc % 2] = tmm
    invD = 1.0 / D
    if mean:
        tmb = P.add("dve", lambda e: e.tensor_scalar(out_mbc, pb1, invD, None,
                    ALU.mult), deps=[t1, tmm])
        tv1 = P.add("dve", lambda e: e.tensor_tensor(out_rbc, out_mbc, out_mbc,
                    ALU.mult), deps=[tmb])
        tv2 = P.add("dve", lambda e: e.scalar_tensor_tensor(out_rbc, pb2, invD,
                    out_rbc, ALU.mult, ALU.subtract), deps=[tv1, tmm])
        t_s_read = tv2
    else:
        tmb = None
        tv2 = P.add("dve", lambda e: e.tensor_scalar(out_rbc, pb2, invD, None,
                    ALU.mult), deps=[tmm])
        t_s_read = tv2
    epscol = ctx["sc"]["cvals"][:, 0:1] if eps == 1e-5 else ctx["sc"]["cvals"][:, 1:2]
    tl = P.add("act", lambda e: e.activation(out_rbc, out_rbc, AF.Ln, bias=epscol),
               deps=[tv2])
    te = P.add("act", lambda e: e.activation(out_rbc, out_rbc, AF.Exp, scale=-0.5),
               deps=[tl])
    return te, tmb, t_s_read


def elu1(ctx, src_ps, dst_bf, scr_a, scr_b, dep, extra=()):
    P = ctx["P"]
    t1 = P.add("dve", lambda e: e.tensor_scalar(scr_a, src_ps, 0.0, None, ALU.min),
               deps=[dep] + list(extra))
    t2 = P.add("act", lambda e: e.activation(scr_b, scr_a, AF.Exp), deps=[t1])
    t3 = P.add("dve", lambda e: e.scalar_tensor_tensor(dst_bf, src_ps, 0.0, scr_b,
               ALU.max, ALU.add), deps=[t2])
    return t3


def emit_m8(ctx, s01, m8, scr, deps):
    P = ctx["P"]
    t0 = P.add("act", lambda e: e.activation(m8, s01, AF.Abs), deps=deps)
    t1 = P.add("act", lambda e: e.activation(scr, s01, AF.Sign), deps=[t0])
    t2 = P.add("dve", lambda e: e.tensor_scalar(m8, m8, 0.5, MAGIC, ALU.subtract,
               ALU.add), deps=[t0])
    t3 = P.add("dve", lambda e: e.tensor_scalar(m8, m8, MAGIC, None, ALU.subtract),
               deps=[t2])
    t4 = P.add("dve", lambda e: e.tensor_tensor(m8, m8, scr, ALU.mult),
               deps=[t3, t1])
    t5 = P.add("dve", lambda e: e.tensor_scalar(scr, m8, 0.125, 0.4375, ALU.mult,
               ALU.subtract), deps=[t4])
    t6 = P.add("dve", lambda e: e.tensor_scalar(scr, scr, MAGIC, MAGIC, ALU.add,
               ALU.subtract), deps=[t5])
    t7 = P.add("dve", lambda e: e.tensor_scalar(scr, scr, -8.0, None, ALU.mult),
               deps=[t6])
    t8 = P.add("dve", lambda e: e.tensor_tensor(m8, m8, scr, ALU.add), deps=[t7])
    return t8


def emit_tin(ctx):
    P, AR, sc, PB = ctx["P"], ctx["AR"], ctx["sc"], ctx["PB"]
    NT512 = ctx["NT512"]
    x_in, xs = ctx["x_in"], ctx["xs"]
    xsr = xs.rearrange("(f p) t -> p f t", p=128)
    AR.push()
    xtm = [AR.alloc(4 * D) for _ in range(2)]
    xfm = [AR.alloc(FC * 512) for _ in range(2)]
    ident = sc["ident_f"]
    t_tp_last = [None, None]
    t_st_last = [None, None]
    t_ev = {0: None, 1: None}
    for t in range(NT512):
        bi = t % 2
        buf = xtm[bi].rearrange("p (s d) -> p s d", s=4)
        tl = P.dma("sp", lambda e, b=buf, t=t: e.dma_start(
            out=b, in_=x_in[t * 512:(t + 1) * 512, :].rearrange(
                "(s p) d -> p s d", p=128)), deps=[t_tp_last[bi]])
        fb = xfm[bi].rearrange("p (f t) -> p f t", f=FC)
        evs = []
        for fc in range(FC):
            bk = fc % 2
            fns = [(lambda e, s=s, fc=fc, b=buf, bk=bk: e.transpose(
                PB[bk][:, s * 128:(s + 1) * 128],
                b[:, s, fc * 128:(fc + 1) * 128], ident)) for s in range(4)]
            tm = P.add("pe", fns, deps=[tl, t_ev[bk]])
            if fc == FC - 1:
                t_tp_last[bi] = tm
            t_ev[bk] = P.add("dve", lambda e, fc=fc, bk=bk, fb=fb: e.tensor_copy(
                fb[:, fc, :], PB[bk][:, 0:512]), deps=[tm, t_st_last[bi]])
            evs.append(t_ev[bk])
        t_st_last[bi] = P.dma("sp", lambda e, fb=fb, t=t: e.dma_start(
            out=xsr[:, :, t * 512:(t + 1) * 512], in_=fb), deps=evs)
    AR.pop()


def emit_tout(ctx):
    P, AR, sc, PB = ctx["P"], ctx["AR"], ctx["sc"], ctx["PB"]
    NT512 = ctx["NT512"]
    xs, y_out = ctx["xs"], ctx["y_out"]
    xsr = xs.rearrange("(f p) t -> p f t", p=128)
    AR.push()
    xfm = [AR.alloc(FC * 512) for _ in range(2)]
    ytm = [AR.alloc(4 * D) for _ in range(2)]
    ident = sc["ident_f"]
    t_rd_last = [None, None]
    t_st_last = [None, None]
    t_ev = {0: None, 1: None}
    for t in range(NT512):
        bi = t % 2
        fb = xfm[bi].rearrange("p (f t) -> p f t", f=FC)
        tl = P.dma("sp", lambda e, fb=fb, t=t: e.dma_start(
            out=fb, in_=xsr[:, :, t * 512:(t + 1) * 512]), deps=[t_rd_last[bi]])
        ob = ytm[bi].rearrange("p (s d) -> p s d", s=4)
        evs = []
        for s in range(4):
            for half in range(2):
                bk = (2 * s + half) % 2
                fns = [(lambda e, fc=fc, s=s, fb=fb, bk=bk: e.transpose(
                    PB[bk][:, (fc % 4) * 128:(fc % 4 + 1) * 128],
                    fb[:, fc, s * 128:(s + 1) * 128], ident))
                    for fc in range(half * 4, half * 4 + 4)]
                tm = P.add("pe", fns, deps=[tl, t_ev[bk]])
                if s == 3 and half == 1:
                    t_rd_last[bi] = tm
                t_ev[bk] = P.add("dve", lambda e, s=s, half=half, ob=ob, bk=bk:
                                 e.tensor_copy(ob[:, s, half * 512:(half + 1) * 512],
                                               PB[bk][:, 0:512]),
                                 deps=[tm, t_st_last[bi]])
                evs.append(t_ev[bk])
        t_st_last[bi] = P.dma("sp", lambda e, ob=ob, t=t: e.dma_start(
            out=y_out[t * 512:(t + 1) * 512, :].rearrange("(s p) d -> p s d", p=128),
            in_=ob), deps=evs)
    AR.pop()


def emit_moe(ctx):
    P, AR, sc, PB = ctx["P"], ctx["AR"], ctx["sc"], ctx["PB"]
    LH, NHT = ctx["LH"], ctx["NHT"]
    xs, fW1, fW2, b1c, fb2 = ctx["xs"], ctx["fW1"], ctx["fW2"], ctx["b1c"], ctx["fb2"]
    b2pk = ctx["b2pk"]
    xsr = xs.rearrange("(f p) t -> p f t", p=128)
    NQT = LH // 256

    AR.push()
    x_half = AR.alloc(FC * LH).rearrange("p (f t) -> p f t", f=FC)
    ln2 = AR.alloc(FC * LH, BF16).rearrange("p (f t) -> p f t", f=FC)
    b1sb = AR.alloc(E * 32)
    w1b = [AR.alloc(FC * 1024, BF16).rearrange("p (f q) -> p f q", f=FC)
           for _ in range(2)]
    w2b = [AR.alloc(FC * 1024, BF16).rearrange("p (d q) -> p d q", d=FC)
           for _ in range(2)]
    hsb = [AR.alloc(512, BF16) for _ in range(8)]
    hsc = [AR.alloc(512, BF16) for _ in range(3)]
    scr = [AR.alloc(512) for _ in range(2)]
    asc = [AR.alloc(512), None]
    mbc = AR.alloc(512)
    rbc = AR.alloc(512)
    wbc = AR.alloc((LH // 512) * 512, BF16)
    s01r = AR.alloc(512)[0:1, :]
    m8r = AR.alloc(512)[0:1, :]
    wm8 = AR.alloc(LH, BF16)[:8, :]
    wsc = AR.alloc(512)
    wsc2 = AR.alloc(512)
    rscr = wsc2[0:1, :]

    tb1 = P.dma("sp", lambda e: e.dma_start(out=b1sb, in_=b1c[:]))

    for half in range(2):
        hsl = slice(half * LH, (half + 1) * LH)
        t_loads = []
        for t4i in range(NHT):
            t_loads.append(P.dma("sp", lambda e, half=half, t4i=t4i:
                e.dma_start(out=x_half[:, :, t4i * 512:(t4i + 1) * 512],
                            in_=xsr[:, :, half * LH + t4i * 512:
                                    half * LH + (t4i + 1) * 512])))
        tload = t_loads[-1]
        t_lnapply = [None, None]
        t_routing = []
        t_pb67 = t_loads[0]
        t_m8rd = None
        for t4i in range(NHT):
            tsl = slice(t4i * 512, (t4i + 1) * 512)
            te, tmb, tsr = norm_stats(
                ctx, lambda fc, tsl=tsl: x_half[:, fc, tsl], PB[6][:, 0:512],
                PB[7][:, 0:512], scr[0], scr[1], mbc, rbc,
                [t_loads[t4i], t_pb67], 1e-5, mean=True)
            tp = P.add("pe", lambda e, tsl=tsl: e.matmul(
                PB[6][0:1, 0:512], sc["e01"], x_half[:, 0, tsl],
                start=True, stop=True),
                deps=[tmb, tsr, t_routing[-1] if t_routing else None,
                      t_loads[t4i]])
            ta = P.add("dve", lambda e: e.tensor_scalar(
                s01r, mbc[0:1, 0:512], -2.0, None, ALU.mult),
                deps=[tmb, t_routing[-1] if t_routing else None])
            tb = P.add("dve", lambda e: e.tensor_tensor(
                s01r, PB[6][0:1, 0:512], s01r, ALU.add), deps=[ta, tp])
            tc = P.add("dve", lambda e: e.tensor_tensor(
                s01r, s01r, rbc[0:1, 0:512], ALU.mult), deps=[tb, te])
            t_pb67 = tb
            tm8 = emit_m8(ctx, s01r, m8r, rscr,
                          [tc, t_routing[-1] if t_routing else None, t_m8rd])
            tbc = P.add("pe", lambda e: e.matmul(
                PB[6][0:8, 0:512], sc["ones_f"][0:1, 0:8], m8r,
                start=True, stop=True), deps=[tm8, tb])
            t_m8rd = tbc
            t1 = P.add("dve", lambda e: e.tensor_scalar(
                wsc[0:8, 0:512], PB[6][0:8, 0:512], sc["iota8c"][0:8, :], None,
                ALU.subtract), deps=[tbc, t_routing[-1] if t_routing else None])
            t2 = P.add("act", lambda e: e.activation(
                wsc[0:8, 0:512], wsc[0:8, 0:512], AF.Abs), deps=[t1])
            t3 = P.add("act", lambda e: e.activation(
                wsc[0:8, 0:512], wsc[0:8, 0:512], AF.Relu, scale=-1.0,
                bias=sc["cvals"][0:8, 2:3]), deps=[t2])
            t4 = P.add("dve", lambda e: e.tensor_scalar(
                wsc2[0:8, 0:512], PB[6][0:8, 0:512], sc["c8c"][0:8, :], None,
                ALU.subtract), deps=[tbc, t3])
            t5 = P.add("act", lambda e: e.activation(
                wsc2[0:8, 0:512], wsc2[0:8, 0:512], AF.Abs), deps=[t4])
            t6 = P.add("act", lambda e: e.activation(
                wsc2[0:8, 0:512], wsc2[0:8, 0:512], AF.Relu, scale=-1.0,
                bias=sc["cvals"][0:8, 2:3]), deps=[t5])
            t7 = P.add("dve", lambda e: e.tensor_tensor(
                wsc[0:8, 0:512], wsc[0:8, 0:512], wsc2[0:8, 0:512], ALU.add),
                deps=[t6, t3])
            t8 = P.add("dve", lambda e, t4_=t4i: e.tensor_scalar(
                wm8[:, t4_ * 512:(t4_ + 1) * 512], wsc[0:8, 0:512], 0.5, None,
                ALU.mult), deps=[t7])
            t_pb67 = t4
            t_routing.append(t8)
            for fc in range(FC):
                tA = P.add("dve", lambda e, fc=fc, tsl=tsl: e.tensor_tensor(
                    asc[0], x_half[:, fc, tsl], mbc, ALU.subtract),
                    deps=[tmb, t_lnapply[0]])
                tB = P.add("dve", lambda e, fc=fc, tsl=tsl: e.tensor_tensor(
                    ln2[:, fc, tsl], asc[0], rbc, ALU.mult), deps=[tA, te])
                t_lnapply[0] = tB
                t_lnapply[1] = tB
        t_ln_done = P.add("dve", lambda e: e.engine_nop(),
                          deps=t_lnapply + t_routing)
        t_wm8_done = t_ln_done
        dbg = ctx.get("dbg") or {}
        if dbg:
            P.dma("sp", lambda e, hsl=hsl: e.dma_start(
                out=dbg["ln2"].rearrange("(f p) t -> p f t", p=128)[:, :, hsl],
                in_=ln2), deps=[t_ln_done])
            P.dma("sp", lambda e, hsl=hsl: e.dma_start(
                out=dbg["wm8"][0:8, hsl], in_=wm8), deps=[t_wm8_done])
            t_dbgx = P.dma("sp", lambda e, hsl=hsl: e.dma_start(
                out=dbg["xpre"].rearrange("(f p) t -> p f t", p=128)[:, :, hsl],
                in_=x_half), deps=[t_ln_done])

        t_pb6 = t_pb67
        tw_load = {}
        tw_free = {}

        def load_piece(eq):
            ei, q = divmod(eq, 4)
            bi = eq % 2
            dep = [tw_free.get(eq - 2), tb1 if eq == 0 else None]
            t1 = P.dma("pool", lambda e, ei=ei, q=q, bi=bi: e.dma_start(
                out=w1b[bi], in_=fW1[ei].rearrange("(f p) d -> p f d", p=128)
                [:, :, q * 1024:(q + 1) * 1024]), deps=dep)
            t2 = P.dma("pool", lambda e, ei=ei, q=q, bi=bi: e.dma_start(
                out=w2b[bi], in_=fW2[ei][q * 1024:(q + 1) * 1024, :]
                .rearrange("(d p) o -> p d o", p=128)), deps=[t1])
            tw_load[eq] = (t1, t2)

        load_piece(0)
        load_piece(1)
        NT5 = LH // 512
        t_wbc_wr = [None] * NT5
        t_wbc_rd = [None] * NT5
        t_pair_free = {0: (dbg and t_dbgx or None), 1: (dbg and t_dbgx or None)}
        HB = [0, 1, 7]
        t_hid_rd = {0: t_pb67, 1: t_pb67, 7: t_pb67}
        t_hsc_rd = [None, None, None]
        t_hsb_mm2 = [None] * 8
        t_xadd_last = dbg and t_dbgx or None

        def emit_wbc(ei):
            nonlocal t_pb6
            for t in range(NT5):
                tp = P.add("pe", lambda e, t=t, ei=ei: e.matmul(
                    PB[6][:, 0:512], sc["sel8"][:, ei * 128:(ei + 1) * 128],
                    wm8[:, t * 512:(t + 1) * 512], start=True, stop=True),
                    deps=[t_wm8_done, t_wbc_wr[t], t_pb6])
                t_wbc_wr[t] = P.add("dve", lambda e, t=t: e.tensor_copy(
                    wbc[:, t * 512:(t + 1) * 512], PB[6][:, 0:512]),
                    deps=[tp, t_wbc_rd[t]])
                t_pb6 = t_wbc_wr[t]

        for ei in range(E):
            emit_wbc(ei)
            for q in range(4):
                eq = ei * 4 + q
                tw1, tw2 = tw_load[eq]
                for t in range(NT5):
                    tsl = slice(t * 512, (t + 1) * 512)
                    t_e2 = [None] * 8
                    tq_last = {}

                    def qmms(xq, dc):
                        pair = xq % 2
                        bank0 = 2 + 2 * pair
                        fns = []
                        for j in range(2):
                            oc = 2 * xq + j
                            pbank = PB[bank0 + j]
                            fns.append(lambda e, pbank=pbank, oc=oc, dc=dc,
                                       bi=eq % 2: e.matmul(
                                pbank[:, 0:512],
                                w2b[bi][:, dc, oc * 128:(oc + 1) * 128],
                                hsb[dc], start=(dc == 0), stop=(dc == 7)))
                        extra = None
                        if ei == 0 and q == 0 and dc == 0:
                            for j in range(2):
                                oc = 2 * xq + j
                                pbank = PB[bank0 + j]
                                fns.append(lambda e, pbank=pbank, oc=oc, t=t:
                                           e.matmul(pbank[:, 0:512],
                                            b2pk[0:8, oc * 128:(oc + 1) * 128],
                                            wm8[:, t * 512:(t + 1) * 512],
                                            start=False, stop=False))
                            extra = ctx["t_b2pk"]
                        tm = P.add("pe", fns, deps=[
                            tw2, t_e2[dc],
                            t_pair_free[pair] if dc == 0 else None,
                            t_wm8_done, extra, tq_last.get(xq)])
                        tq_last[xq] = tm
                        t_hsb_mm2[dc] = tm
                        return tm

                    def qadds(xq):
                        pair = xq % 2
                        bank0 = 2 + 2 * pair
                        prev = [tq_last[xq], t_lnapply[0], t_lnapply[1],
                                t_xadd_last]
                        for j in range(2):
                            oc = 2 * xq + j
                            ta2 = P.add("dve", lambda e, oc=oc, j=j, tsl=tsl,
                                        bank0=bank0: e.tensor_tensor(
                                x_half[:, oc, tsl], x_half[:, oc, tsl],
                                PB[bank0 + j][:, 0:512], ALU.add), deps=prev)
                            prev = [ta2]
                        t_pair_free[pair] = prev[0]
                        return prev[0]

                    for dc in range(8):
                        b3 = dc % 3
                        hb = HB[b3]
                        def mk1(fc, dc=dc, bi=eq % 2, hb=hb, tsl=tsl):
                            return lambda e: e.matmul(
                                PB[hb][:, 0:512],
                                w1b[bi][:, fc, dc * 128:(dc + 1) * 128],
                                ln2[:, fc, tsl],
                                start=(fc == 0), stop=(fc == FC - 1))
                        tm1 = P.add("pe", [mk1(fc) for fc in range(FC)],
                                    deps=[tw1, t_ln_done, t_hid_rd[hb]])
                        bcol = b1sb[:, ei * 32 + q * 8 + dc:
                                    ei * 32 + q * 8 + dc + 1]
                        te1 = P.add("act", lambda e, hb=hb, b3=b3, bcol=bcol:
                                    e.activation(hsc[b3], PB[hb][:, 0:512],
                                                 AF.Relu, bias=bcol),
                                    deps=[tm1, t_hsc_rd[b3]])
                        t_hid_rd[hb] = te1
                        te2 = P.add("dve", lambda e, b3=b3, dc=dc, t=t:
                                    e.tensor_tensor(
                            hsb[dc], hsc[b3], wbc[:, t * 512:(t + 1) * 512],
                            ALU.mult), deps=[te1, t_wbc_wr[t], t_hsb_mm2[dc]])
                        t_hsc_rd[b3] = te2
                        t_wbc_rd[t] = te2
                        t_e2[dc] = te2
                        if dc >= 2:
                            qmms(0, dc - 2)
                    qmms(0, 6)
                    qmms(0, 7)
                    for dc in range(8):
                        qmms(1, dc)
                    t_xadd_last = qadds(0)
                    for dc in range(8):
                        qmms(2, dc)
                    t_xadd_last = qadds(1)
                    for dc in range(8):
                        qmms(3, dc)
                    t_xadd_last = qadds(2)
                    t_xadd_last = qadds(3)
                tw_free[eq] = t_xadd_last
                if eq + 2 < E * 4:
                    load_piece(eq + 2)
        t_out_free = [t_xadd_last]
        tst = P.dma("sp", lambda e, hsl=hsl: e.dma_start(
            out=xsr[:, :, hsl], in_=x_half), deps=t_out_free)
        P.barrier()
    AR.pop()


def emit_attna(ctx):
    P, AR, sc, PB = ctx["P"], ctx["AR"], ctx["sc"], ctx["PB"]
    L, NT512 = ctx["L"], ctx["NT512"]
    xs, aW, ln1s, ktvs = ctx["xs"], ctx["aW"], ctx["ln1s"], ctx["ktvs"]
    xsr = xs.rearrange("(f p) t -> p f t", p=128)
    ln1r = ln1s.rearrange("(f p) t -> p f t", p=128)
    NT128 = L // 128

    AR.push()
    wa = AR.alloc(FC * 3 * D, BF16).rearrange("p (f c) -> p f c", f=FC)
    kvb = AR.alloc(FC * D, BF16).rearrange("p (d h) -> p d h", d=FC)
    ksum = AR.alloc(8, BF16)
    tw = P.dma("pool", lambda e: e.dma_start(
        out=wa, in_=aW.rearrange("(f p) c -> p f c", p=128)))

    AR.push()
    xin = [AR.alloc(FC * 512).rearrange("p (f t) -> p f t", f=FC) for _ in range(2)]
    ln1t = [AR.alloc(FC * 512, BF16).rearrange("p (f t) -> p f t", f=FC)
            for _ in range(2)]
    ktvt = [AR.alloc(4 * 2 * D, BF16).rearrange("p (s c) -> p s c", s=4)
            for _ in range(2)]
    scr = [AR.alloc(512) for _ in range(4)]
    scrE = [AR.alloc(512) for _ in range(2)]
    mbc = AR.alloc(512)
    rbc = AR.alloc(512)
    t_xin_rd = [None, None]
    t_ln_st = [None, None]
    t_ktv_st = [None, None]
    t_pb01 = None
    t_kv_ps = {}
    t_kv_evac = {}
    tln = [None, None]
    for t in range(NT512):
        bi = t % 2
        tsl = slice(t * 512, (t + 1) * 512)
        tl = P.dma("sp", lambda e, bi=bi, tsl=tsl: e.dma_start(
            out=xin[bi], in_=xsr[:, :, tsl]), deps=[t_xin_rd[bi]])
        te, tmb, tsr = norm_stats(
            ctx, lambda fc, bi=bi: xin[bi][:, fc, :], PB[0][:, 0:512],
            PB[1][:, 0:512], scr[0], scr[1], mbc, rbc, [tl, t_pb01],
            1e-5, mean=True)
        t_pb01 = tsr
        for fc in range(FC):
            p = fc % 2
            tA = P.add("dve", lambda e, fc=fc, bi=bi, p=p: e.tensor_tensor(
                scr[2 + p], xin[bi][:, fc, :], mbc, ALU.subtract),
                deps=[tmb, tln[p]])
            tB = P.add("dve", lambda e, fc=fc, bi=bi, p=p: e.tensor_tensor(
                ln1t[bi][:, fc, :], scr[2 + p], rbc, ALU.mult),
                deps=[tA, te, t_ln_st[bi]])
            tln[p] = tB
        t_xin_rd[bi] = tln[0]
        t_ln_st[bi] = P.dma("sp", lambda e, bi=bi, tsl=tsl: e.dma_start(
            out=ln1r[:, :, tsl], in_=ln1t[bi]), deps=[tln[0], tln[1]])
        ev_last = None
        for s in range(4):
            mm_ts = []
            for n in range(4):
                bank = PB[2 + n]
                def mkkv(fc, s=s, n=n, bi=bi, bank=bank):
                    return lambda e: e.matmul(
                        bank[:, 0:512], ln1t[bi][:, fc, s * 128:(s + 1) * 128],
                        wa[:, fc, D + n * 512:D + (n + 1) * 512],
                        start=(fc == 0), stop=(fc == FC - 1))
                mm_ts.append(P.add("pe", [mkkv(fc) for fc in range(FC)],
                                   deps=[tw, tln[0], tln[1], t_kv_ps.get(2 + n)]))
            for n in range(2):
                tk = elu1(ctx, PB[2 + n][:, 0:512],
                          ktvt[bi][:, s, n * 512:(n + 1) * 512],
                          scrE[0], scrE[1], mm_ts[n],
                          extra=[t_ktv_st[bi], ev_last])
                t_kv_ps[2 + n] = tk
                ev_last = tk
            for n in range(2, 4):
                tv = P.add("dve", lambda e, n=n, s=s, bi=bi: e.tensor_copy(
                    ktvt[bi][:, s, n * 512:(n + 1) * 512], PB[2 + n][:, 0:512]),
                    deps=[mm_ts[n], t_ktv_st[bi]])
                t_kv_ps[2 + n] = tv
                ev_last = tv
        t_ktv_st[bi] = P.dma("sp", lambda e, bi=bi, t=t: e.dma_start(
            out=ktvs[t * 512:(t + 1) * 512, :].rearrange("(s p) c -> p s c", p=128),
            in_=ktvt[bi]), deps=[t_kv_ps[2], t_kv_ps[3], t_kv_ps[4], t_kv_ps[5]])
    tA_done = P.barrier()
    AR.pop()

    AR.push()
    ktvt2 = [AR.alloc(2 * D, BF16) for _ in range(2)]
    passes = [(0, 3, [0, 1]), (3, 6, [2, 3]), (6, 8, [4, 5, 6, 7])]
    t_ld_rd = [None, None]
    t_ks_evac = {}
    t_kv_evac = dict(t_kv_evac)
    for (d0, d1, ks_dcs) in passes:
        nkv = d1 - d0
        ks_banks = list(range(2 * nkv, 2 * nkv + len(ks_dcs)))
        mm_last = {}
        ks_last = {}
        for lt in range(NT128):
            bi = lt % 2
            tl = P.dma("sp", lambda e, bi=bi, lt=lt: e.dma_start(
                out=ktvt2[bi], in_=ktvs[lt * 128:(lt + 1) * 128, :]),
                deps=[t_ld_rd[bi]])
            last_pe = None
            for j, dc in enumerate(range(d0, d1)):
                for nh in range(2):
                    bank = PB[2 * j + nh]
                    tmm = P.add("pe", lambda e, dc=dc, nh=nh, bi=bi, lt=lt,
                                bank=bank: e.matmul(
                        bank[:, 0:512], ktvt2[bi][:, dc * 128:(dc + 1) * 128],
                        ktvt2[bi][:, D + nh * 512:D + (nh + 1) * 512],
                        start=(lt == 0), stop=(lt == NT128 - 1)),
                        deps=[tl, mm_last.get((j, nh)),
                              t_kv_evac.get(2 * j + nh) if lt == 0 else None])
                    mm_last[(j, nh)] = tmm
                    last_pe = tmm
            for j, dc in enumerate(ks_dcs):
                bk = ks_banks[j]
                tks = P.add("pe", lambda e, dc=dc, bi=bi, lt=lt, bk=bk: e.matmul(
                    PB[bk][:, 0:1], ktvt2[bi][:, dc * 128:(dc + 1) * 128],
                    sc["ones_b"][:, 0:1],
                    start=(lt == 0), stop=(lt == NT128 - 1)),
                    deps=[tl, ks_last.get(j),
                          t_kv_evac.get(bk) if lt == 0 else None,
                          t_ks_evac.get(bk) if lt == 0 else None])
                ks_last[j] = tks
                last_pe = tks
            t_ld_rd[bi] = last_pe
        for j, dc in enumerate(range(d0, d1)):
            for nh in range(2):
                tkv = P.add("dve", lambda e, dc=dc, nh=nh, j=j: e.tensor_copy(
                    kvb[:, dc, nh * 512:(nh + 1) * 512], PB[2 * j + nh][:, 0:512]),
                    deps=[mm_last[(j, nh)]])
                t_kv_evac[2 * j + nh] = tkv
        for j, dc in enumerate(ks_dcs):
            bk = ks_banks[j]
            tks = P.add("dve", lambda e, dc=dc, bk=bk: e.tensor_copy(
                ksum[:, dc:dc + 1], PB[bk][:, 0:1]), deps=[ks_last[j]])
            t_ks_evac[bk] = tks
    tA2 = P.barrier()
    AR.pop()

    AR.push()
    ln1t2 = [AR.alloc(FC * 512, BF16).rearrange("p (f t) -> p f t", f=FC)
             for _ in range(2)]
    xin2 = [AR.alloc(FC * 512).rearrange("p (f t) -> p f t", f=FC)
            for _ in range(2)]
    qbf = AR.alloc(FC * 512, BF16).rearrange("p (d t) -> p d t", d=FC)
    rbc2 = AR.alloc(512)
    rrow = AR.alloc(512)
    scr2 = [AR.alloc(512) for _ in range(3)]
    t_ln_rd = [None, None]
    t_x_rd = [None, None]
    t_q_ps = [None, None]
    t_q_rd = None
    t_den_ps = None
    t_rbc_ps = None
    t_v_ps = [None, None]
    t_x_st = [None, None]
    for t in range(NT512):
        bi = t % 2
        tsl = slice(t * 512, (t + 1) * 512)
        tl = P.dma("sp", lambda e, bi=bi, tsl=tsl: e.dma_start(
            out=ln1t2[bi], in_=ln1r[:, :, tsl]), deps=[t_ln_rd[bi]])
        tx = P.dma("sp", lambda e, bi=bi, tsl=tsl: e.dma_start(
            out=xin2[bi], in_=xsr[:, :, tsl]), deps=[t_x_rd[bi], t_x_st[bi]])
        t_q = []
        for qc in range(FC):
            hb = qc % 2
            tm = P.add("pe", [
                (lambda e, fc=fc, qc=qc, bi=bi, hb=hb: e.matmul(
                    PB[hb][:, 0:512], wa[:, fc, qc * 128:(qc + 1) * 128],
                    ln1t2[bi][:, fc, :], start=(fc == 0), stop=(fc == FC - 1)))
                for fc in range(FC)],
                deps=[tw, tl, t_q_ps[hb]])
            tq = elu1(ctx, PB[hb][:, 0:512], qbf[:, qc, :], scr2[0], scr2[1],
                      tm, extra=[t_q_rd])
            t_q_ps[hb] = tq
            t_q.append(tq)
        tden = P.add("pe", [
            (lambda e, dc=dc: e.matmul(
                PB[6][0:1, 0:512], ksum[:, dc:dc + 1], qbf[:, dc, :],
                start=(dc == 0), stop=(dc == FC - 1)))
            for dc in range(FC)],
            deps=t_q + [t_den_ps, tA2["pe"]])
        tr1 = P.add("dve", lambda e: e.tensor_scalar(
            rrow[0:1, 0:512], PB[6][0:1, 0:512], 1e-6, None, ALU.add),
            deps=[tden])
        tr2 = P.add("dve", lambda e: e.reciprocal(
            rrow[0:1, 0:512], rrow[0:1, 0:512]), deps=[tr1])
        t_den_ps = tr1
        trb = P.add("pe", lambda e: e.matmul(
            PB[7][:, 0:512], sc["ones_f"][0:1, :], rrow[0:1, 0:512],
            start=True, stop=True), deps=[tr2, t_rbc_ps])
        trc = P.add("dve", lambda e: e.tensor_copy(rbc2, PB[7][:, 0:512]),
                    deps=[trb])
        t_rbc_ps = trc
        t_upd = None
        t_lastvmm = None
        for hc in range(FC):
            vb = 2 + hc % 2
            tm = P.add("pe", [
                (lambda e, dc=dc, hc=hc, vb=vb: e.matmul(
                    PB[vb][:, 0:512], kvb[:, dc, hc * 128:(hc + 1) * 128],
                    qbf[:, dc, :], start=(dc == 0), stop=(dc == FC - 1)))
                for dc in range(FC)],
                deps=t_q + [t_v_ps[vb - 2]])
            t_lastvmm = tm
            tv1 = P.add("dve", lambda e, vb=vb: e.tensor_tensor(
                scr2[2], PB[vb][:, 0:512], rbc2, ALU.mult),
                deps=[tm, trc, t_upd])
            t_upd = P.add("dve", lambda e, hc=hc, bi=bi: e.tensor_tensor(
                xin2[bi][:, hc, :], xin2[bi][:, hc, :], scr2[2], ALU.add),
                deps=[tv1, tx])
            t_v_ps[vb - 2] = tv1
        t_q_rd = t_lastvmm
        t_rbc_ps = t_upd
        t_x_rd[bi] = t_upd
        t_x_st[bi] = P.dma("sp", lambda e, bi=bi, tsl=tsl: e.dma_start(
            out=xsr[:, :, tsl], in_=xin2[bi]), deps=[t_upd])
        t_ln_rd[bi] = t_q[-1]
    AR.pop()
    AR.pop()


def emit_attnb(ctx):
    P, AR, sc, PB = ctx["P"], ctx["AR"], ctx["sc"], ctx["PB"]
    L, NT512 = ctx["L"], ctx["NT512"]
    xs, w_qkv, w_out = ctx["xs"], ctx["w_qkv"], ctx["w_out"]
    g1c, g2c, h2s = ctx["g1c"], ctx["g2c"], ctx["h2s"]
    xsr = xs.rearrange("(f p) t -> p f t", p=128)
    h2r = h2s.rearrange("(f p) t -> p f t", p=128)
    xmean = ctx["xmean"]
    cin = ctx["cin"]

    AR.push()
    wqk = AR.alloc(FC * 3 * D, BF16).rearrange("p (f c) -> p f c", f=FC)
    wo = AR.alloc(FC * D, BF16).rearrange("p (r c) -> p r c", r=FC)
    g1s = AR.alloc(FC)
    g2s = AR.alloc(FC)
    S_sb = AR.alloc(H * HD)
    Z_sb = AR.alloc(H)
    S_bf = AR.alloc(H * HD, BF16)
    Z_bf = AR.alloc(H, BF16)
    x2 = AR.alloc(FC * 512).rearrange("p (f t) -> p f t", f=FC)
    ln3 = AR.alloc(FC * 512).rearrange("p (f t) -> p f t", f=FC)
    tfull = ln3
    h1b = AR.alloc(FC * 512, BF16).rearrange("p (f t) -> p f t", f=FC)
    h2b = h1b
    qpad = AR.alloc(H * 512, BF16).rearrange("p (h t) -> p h t", h=H)
    kpad = AR.alloc(H * 512, BF16).rearrange("p (h t) -> p h t", h=H)
    vtm = AR.alloc(4 * D, BF16).rearrange("p (s c) -> p s c", s=4)
    attn = AR.alloc(FC * 512, BF16).rearrange("p (c t) -> p c t", c=FC)
    qeb = [AR.alloc(512, BF16) for _ in range(3)]
    A_bf = AR.alloc(512, BF16)
    atmp = AR.alloc(256, BF16)
    ktm = [AR.alloc(128, BF16) for _ in range(2)]
    rp = [AR.alloc(512) for _ in range(2)]
    scrE = [AR.alloc(512) for _ in range(2)]
    ascr = [AR.alloc(512) for _ in range(2)]
    sscr = [AR.alloc(512) for _ in range(2)]
    mbc = AR.alloc(512)
    rbc = AR.alloc(512)
    rmsb = AR.alloc(512)
    rrow = AR.alloc(256)
    rbcs = AR.alloc(256)
    xmp = AR.alloc(1)
    cosb = [AR.alloc(512, BF16) for _ in range(2)]
    sinb = [AR.alloc(512, BF16) for _ in range(2)]
    rbcs2dbg = AR.alloc(256)
    abfdbg = AR.alloc(512, BF16)
    dnsc = AR.alloc(256)
    pb0dbg = AR.alloc(512)

    tw1 = P.dma("pool", lambda e: e.dma_start(
        out=wqk, in_=w_qkv.rearrange("(f p) c -> p f c", p=128)))
    tw2 = P.dma("pool", lambda e: e.dma_start(
        out=wo, in_=w_out.rearrange("(f p) c -> p f c", p=128)))
    tg1 = P.dma("sp", lambda e: e.dma_start(out=g1s, in_=g1c[:]))
    tg2 = P.dma("sp", lambda e: e.dma_start(out=g2s, in_=g2c[:]))
    tz0 = P.add("dve", lambda e: e.memset(S_sb, 0.0))
    tz1 = P.add("dve", lambda e: e.memset(Z_sb, 0.0))
    tz2 = P.add("dve", lambda e: e.memset(qpad, 0.0))
    tz3 = P.add("dve", lambda e: e.memset(kpad, 0.0))
    tz4 = P.add("dve", lambda e: e.memset(S_bf, 0.0))
    tz5 = P.add("dve", lambda e: e.memset(Z_bf, 0.0))
    tz6 = P.add("dve", lambda e: e.memset(xmean, 0.0))

    t_x_st = None
    t_pb = {i: None for i in range(8)}
    t_ln3_rd = None
    t_x2_rd = None
    t_h1_rd = None
    t_qp_rd = None
    t_kp_rd = None
    t_vt_rd = None
    t_attn_rd = None
    t_S_rd = tz0
    t_Z_rd = tz1
    t_Sbf_rd = tz4
    t_Zbf_rd = tz5
    t_S_wr = tz0
    t_Z_wr = tz1
    t_qeb_rd = [None, None, None]
    t_Abf_rd = None
    t_atmp_rd = None
    t_ktm_rd = [None, None]
    t_rbcs_rd = None
    t_ktm_rd = [None, None]
    t_rbcs_rd = None
    t_h2_st = None
    t_xm_rd = tz6

    t_cs_rd = [None, None]
    for t in range(NT512):
        bi = t % 2
        tsl = slice(t * 512, (t + 1) * 512)
        tx = P.dma("sp", lambda e, tsl=tsl: e.dma_start(
            out=x2, in_=xsr[:, :, tsl]), deps=[t_x_st, t_x2_rd])
        tcs = P.dma("sp", lambda e, tsl=tsl, bi=bi: e.dma_start(
            out=cosb[bi], in_=cin["cos2"][:, tsl]), deps=[t_cs_rd[bi]])
        tss = P.dma("sp", lambda e, tsl=tsl, bi=bi: e.dma_start(
            out=sinb[bi], in_=cin["sin2"][:, tsl]), deps=[t_cs_rd[bi]])
        te, tmb, tsr = norm_stats(
            ctx, lambda fc: x2[:, fc, :], PB[0][:, 0:512], PB[1][:, 0:512],
            sscr[0], sscr[1], mbc, rbc, [tx, t_pb[0], t_pb[1]], 1e-5, mean=True)
        t_pb[0] = tmb
        t_pb[1] = tsr
        tla = [None, None]
        for fc in range(FC):
            p = fc % 2
            tA = P.add("dve", lambda e, fc=fc, p=p: e.tensor_tensor(
                ascr[p], x2[:, fc, :], mbc, ALU.subtract),
                deps=[tmb, tla[p], t_ln3_rd])
            tB = P.add("dve", lambda e, fc=fc, p=p: e.tensor_tensor(
                ln3[:, fc, :], ascr[p], rbc, ALU.mult), deps=[tA, te])
            tla[p] = tB
        t_ln3 = [tla[0], tla[1]]
        t_xadd = []
        prev = None
        for fc in range(FC):
            prev = P.add("dve", lambda e, fc=fc: e.tensor_tensor(
                x2[:, fc, :], x2[:, fc, :], ln3[:, fc, :], ALU.add),
                deps=t_ln3 + [prev])
            t_xadd.append(prev)
        t_x2_w = t_xadd[-1]
        tre, _, tsr2 = norm_stats(
            ctx, lambda fc: ln3[:, fc, :], None, PB[1][:, 0:512],
            sscr[0], sscr[1], None, rmsb, t_ln3 + [t_pb[1]], 1e-6, mean=False)
        t_pb[1] = tsr2
        th1 = [None, None]
        for fc in range(FC):
            p = fc % 2
            tA = P.add("dve", lambda e, fc=fc, p=p: e.tensor_tensor(
                ascr[p], ln3[:, fc, :], rmsb, ALU.mult),
                deps=[tre, th1[p]])
            tB = P.add("dve", lambda e, fc=fc, p=p: e.tensor_scalar(
                h1b[:, fc, :], ascr[p], g1s[:, fc:fc + 1], None, ALU.mult),
                deps=[tA, tg1, t_h1_rd, t_h2_st])
            th1[p] = tB
        t_h1 = [th1[0], th1[1]]

        t_qp_w = []
        t_kp_w = []
        qk_ps = {}
        t_scrE = None
        t_rp = None

        def qk_fin1(p):
            nonlocal t_scrE, t_rp, t_qeb_rd
            bank, tm = qk_ps[p]
            q3 = p % 3
            tc1 = P.add("dve", lambda e, bank=bank, q3=q3: e.tensor_copy(
                qeb[q3], bank[:, 0:512]), deps=[tm, t_qeb_rd[q3]])
            trot = P.add("pe", lambda e, q3=q3: e.matmul(
                PB[5][:, 0:512], sc["rotP"], qeb[q3], start=True, stop=True),
                deps=[tc1, t_pb[5]])
            tr1 = P.add("dve", lambda e, bank=bank, bi=bi: e.tensor_tensor(
                rp[0], bank[:, 0:512], cosb[bi], ALU.mult),
                deps=[tm, tc1, tcs, t_rp])
            t_pb_set(bank, tr1)
            tr2 = P.add("dve", lambda e, bi=bi: e.tensor_tensor(
                rp[1], PB[5][:, 0:512], sinb[bi], ALU.mult),
                deps=[trot, tr1, tss])
            t_pb[5] = tr2
            tr3 = P.add("dve", lambda e: e.tensor_tensor(
                rp[0], rp[0], rp[1], ALU.add), deps=[tr2])
            tm1 = P.add("dve", lambda e: e.tensor_scalar(
                scrE[0], rp[0], 0.0, None, ALU.min), deps=[tr3, t_scrE])
            tm2 = P.add("act", lambda e: e.activation(scrE[1], scrE[0], AF.Exp),
                        deps=[tm1])
            tm3 = P.add("dve", lambda e, q3=q3: e.scalar_tensor_tensor(
                qeb[q3], rp[0], 0.0, scrE[1], ALU.max, ALU.add),
                deps=[tm2, trot])
            t_scrE = tm3
            t_rp = tm3
            qk_ps[p] = (bank, tm, tm3)

        def qk_fin2(p):
            nonlocal t_qp_rd, t_kp_rd
            bank, tm, tm3 = qk_ps[p]
            q3 = p % 3
            isq = p < 8
            pc = p if isq else p - 8
            pad = qpad if isq else kpad
            tpe = P.add("dve", lambda e, pad=pad, pc=pc, q3=q3: e.tensor_copy(
                pad[0:64, 2 * pc, :], qeb[q3][0:64, :]),
                deps=[tm3, t_qp_rd if isq else t_kp_rd, tz2, tz3])
            tsh = P.add("pe", lambda e, q3=q3: e.matmul(
                PB[7][0:64, 0:512], sc["shift64"], qeb[q3],
                start=True, stop=True), deps=[tm3, t_pb[7]])
            tpo = P.add("dve", lambda e, pad=pad, pc=pc: e.tensor_copy(
                pad[0:64, 2 * pc + 1, :], PB[7][0:64, 0:512]), deps=[tsh, tpe])
            t_pb[7] = tpo
            t_qeb_rd[q3] = tsh
            if isq:
                t_qp_w.append(tpo)
            else:
                t_kp_w.append(tpo)

        def t_pb_set(bank, tick):
            for i in range(8):
                if bank is PB[i]:
                    t_pb[i] = tick
                    return

        for c16 in range(16):
            bank = PB[2 + c16 % 3]
            tm = P.add("pe", [
                (lambda e, fc=fc, c16=c16, bank=bank: e.matmul(
                    bank[:, 0:512], wqk[:, fc, c16 * 128:(c16 + 1) * 128],
                    h1b[:, fc, :], start=(fc == 0), stop=(fc == FC - 1)))
                for fc in range(FC)],
                deps=t_h1 + [tw1, t_pb[2 + c16 % 3]])
            qk_ps[c16] = (bank, tm)
            if c16 >= 1:
                qk_fin1(c16 - 1)
            if c16 >= 2:
                qk_fin2(c16 - 2)
        qk_fin1(15)
        qk_fin2(14)
        qk_fin2(15)
        t_vt_w = []
        for s in range(4):
            for nh in range(2):
                bank = PB[2 + (2 * s + nh) % 2]
                tm = P.add("pe", [
                    (lambda e, fc=fc, s=s, nh=nh, bank=bank: e.matmul(
                        bank[:, 0:512], h1b[:, fc, s * 128:(s + 1) * 128],
                        wqk[:, fc, 2 * D + nh * 512:2 * D + (nh + 1) * 512],
                        start=(fc == 0), stop=(fc == FC - 1)))
                    for fc in range(FC)],
                    deps=t_h1 + [tw1, t_pb[2 + (2 * s + nh) % 2]])
                tv = P.add("dve", lambda e, s=s, nh=nh, bank=bank: e.tensor_copy(
                    vtm[:, s, nh * 512:(nh + 1) * 512], bank[:, 0:512]),
                    deps=[tm, t_vt_rd])
                t_pb[2 + (2 * s + nh) % 2] = tv
                t_vt_w.append(tv)

        t_attn_w = []
        for ci in range(2):
            gc = 2 * t + ci
            csl = slice(ci * 256, (ci + 1) * 256)
            tsb = P.add("dve", lambda e: e.tensor_copy(S_bf[0:64, :], S_sb[0:64, :]),
                        deps=[t_S_wr, t_Sbf_rd])
            tzb = P.add("dve", lambda e: e.tensor_copy(Z_bf[0:64, :], Z_sb[0:64, :]),
                        deps=[t_Z_wr, t_Zbf_rd])
            t_S_rd = tsb
            t_Z_rd = tzb
            Sbf_r = S_bf.rearrange("p (h d) -> p h d", h=H)
            t_s_upd = []
            t_z_upd = []
            nd_tick = {}
            r_tick = {}
            rsl = [rrow, dnsc]

            def finh(h):
                nonlocal t_rbcs_rd, t_atmp_rd, t_attn_rd
                pc, half = h // 2, h % 2
                nb = 1 if h % 2 == 0 else 3
                tnd_h = nd_tick[h]
                trb = P.add("pe", lambda e, h=h: e.matmul(
                    PB[2][0:64, 256:512], sc["ones_f"][0:1, 0:64],
                    rsl[h % 2][0:1, 0:256], start=True, stop=True),
                    deps=[r_tick[h], t_pb[2]])
                r_rd[h % 2] = trb
                trc = P.add("act", lambda e: e.activation(
                    rbcs[0:64, 0:256], PB[2][0:64, 256:512], AF.Copy),
                    deps=[trb, t_rbcs_rd])
                t_pb[2] = trc
                if half == 0:
                    tat = P.add("dve", lambda e, pc=pc, csl=csl, nb=nb:
                                e.tensor_tensor(
                        attn[0:64, pc, csl], PB[nb][0:64, 0:256],
                        rbcs[0:64, 0:256], ALU.mult),
                        deps=[tnd_h, trc, t_attn_rd])
                else:
                    tat0 = P.add("dve", lambda e, nb=nb: e.tensor_tensor(
                        atmp[0:64, :], PB[nb][0:64, 0:256],
                        rbcs[0:64, 0:256], ALU.mult),
                        deps=[tnd_h, trc, t_atmp_rd])
                    tsu2 = P.add("pe", lambda e: e.matmul(
                        PB[7][:, 0:256], sc["shiftup"], atmp[0:64, :],
                        start=True, stop=True), deps=[tat0, t_pb[7]])
                    tat = P.add("dve", lambda e, pc=pc, csl=csl: e.tensor_copy(
                        attn[64:128, pc, csl], PB[7][64:128, 0:256]),
                        deps=[tsu2, t_attn_rd])
                    t_pb[7] = tat
                    t_atmp_rd = tsu2
                t_pb_num[nb] = tat
                t_rbcs_rd = tat
                nd_tick[("tat", h)] = tat
                tsz = []
                for ph in range(2):
                    ttp = P.add("pe", lambda e, ph=ph, h=h, ci=ci: e.transpose(
                        PB[4][:, 0:256].bitcast(BF16)[:, ph * 128:(ph + 1) * 128],
                        kpad[:, h, ci * 256 + ph * 128:ci * 256 + (ph + 1) * 128],
                        sc["ident_b"]), deps=t_kp_w + [t_pb[4]])
                    tkc = P.add("dve", lambda e, ph=ph: e.tensor_copy(
                        ktm[ph], PB[4][:, 0:256].bitcast(BF16)
                        [:, ph * 128:(ph + 1) * 128]),
                        deps=[ttp, t_ktm_rd[ph]])
                    t_pb[4] = tkc
                    tsz.append(tkc)
                tsu = P.add("pe", [
                    (lambda e, ph=ph, h=h, ci=ci: e.matmul(
                        PB[5][0:64, 0:64], ktm[ph][:, 0:64],
                        vtm[:, 2 * ci + ph, h * 64:(h + 1) * 64],
                        start=(ph == 0), stop=(ph == 1)))
                    for ph in range(2)],
                    deps=tsz + t_vt_w + [t_pb[5]])
                tzu = P.add("pe", [
                    (lambda e, ph=ph: e.matmul(
                        PB[6][0:64, 0:1], ktm[ph][:, 0:64], sc["ones_b"][:, 0:1],
                        start=(ph == 0), stop=(ph == 1)))
                    for ph in range(2)],
                    deps=tsz + [t_pb[6]])
                t_ktm_rd[0] = tzu
                t_ktm_rd[1] = tzu
                tsa = P.add("dve", lambda e, h=h: e.tensor_tensor(
                    S_sb[0:64, h * 64:(h + 1) * 64], S_sb[0:64, h * 64:(h + 1) * 64],
                    PB[5][0:64, 0:64], ALU.add), deps=[tsu, t_S_rd])
                tza = P.add("dve", lambda e, h=h: e.tensor_tensor(
                    Z_sb[0:64, h:h + 1], Z_sb[0:64, h:h + 1], PB[6][0:64, 0:1],
                    ALU.add), deps=[tzu, t_Z_rd, tsa])
                t_pb[5] = tsa
                t_pb[6] = tza
                t_s_upd.append(tsa)
                t_z_upd.append(tza)

            t_pb_num = {1: t_pb[1], 3: t_pb[3]}
            r_rd = {0: None, 1: None}
            for h in range(H):
                pc, half = h // 2, h % 2
                nb = 1 if h % 2 == 0 else 3
                tA1 = P.add("pe", [
                    (lambda e, ph=ph, h=h: e.matmul(
                        PB[0][:, ph * 256:(ph + 1) * 256],
                        kpad[:, h, ci * 256 + ph * 128:ci * 256 + (ph + 1) * 128],
                        qpad[:, h, csl], start=(ph == 0), stop=(ph == 1)))
                    for ph in range(2)],
                    deps=t_kp_w + t_qp_w + [t_pb[0]])
                tmask = P.add("dve", lambda e: e.tensor_tensor(
                    A_bf, PB[0][:, 0:512], sc["tril"], ALU.mult),
                    deps=[tA1, t_Abf_rd])
                t_pb[0] = tmask
                num_fns = [
                    lambda e, h=h, nb=nb: e.matmul(
                        PB[nb][0:64, 0:256], Sbf_r[:, h, :], qpad[:, h, csl],
                        start=True, stop=False)]
                for ph in range(2):
                    num_fns.append(lambda e, ph=ph, h=h, ci=ci, nb=nb: e.matmul(
                        PB[nb][0:64, 0:256], vtm[:, 2 * ci + ph, h * 64:(h + 1) * 64],
                        A_bf[:, ph * 256:(ph + 1) * 256],
                        start=False, stop=(ph == 1)))
                den_fns = [
                    lambda e, h=h: e.matmul(
                        PB[2][0:1, 0:256], Z_bf[:, h:h + 1], qpad[:, h, csl],
                        start=True, stop=False)]
                for ph in range(2):
                    den_fns.append(lambda e, ph=ph: e.matmul(
                        PB[2][0:1, 0:256], sc["ones_b"][:, 0:1],
                        A_bf[:, ph * 256:(ph + 1) * 256],
                        start=False, stop=(ph == 1)))
                tnd = P.add("pe", num_fns + den_fns,
                            deps=[tmask, tsb, tzb, t_pb_num[nb], t_pb[2]])
                nd_tick[h] = tnd
                trl = P.add("act", lambda e, h=h: e.activation(
                    rsl[h % 2][0:1, 0:256], PB[2][0:1, 0:256], AF.Ln,
                    bias=sc["cvals"][0:1, 1:2]),
                    deps=[tnd, r_rd[h % 2]])
                tre = P.add("act", lambda e, h=h: e.activation(
                    rsl[h % 2][0:1, 0:256], rsl[h % 2][0:1, 0:256], AF.Exp,
                    scale=-1.0), deps=[trl])
                r_tick[h] = tre
                t_pb[2] = trl
                t_Abf_rd = tnd
                if h >= 1:
                    finh(h - 1)
            finh(H - 1)
            t_pb[1] = t_pb_num[1]
            t_pb[3] = t_pb_num[3]
            t_S_wr = t_s_upd[-1]
            t_Z_wr = t_z_upd[-1]
            t_Sbf_rd = nd_tick[H - 1]
            t_Zbf_rd = nd_tick[H - 1]
            t_attn_w.append(nd_tick[("tat", H - 1)])
            tnd = nd_tick[H - 1]
        t_qp_rd = tnd
        t_kp_rd = t_pb[4]
        t_vt_rd = t_pb[5]

        t_tf = []
        t_x2_rd_new = None
        t_wout_last = None
        for oc in range(FC):
            bank = PB[oc % 2]
            tm = P.add("pe", [
                (lambda e, rc=rc, oc=oc, bank=bank: e.matmul(
                    bank[:, 0:512], wo[:, rc, oc * 128:(oc + 1) * 128],
                    attn[:, rc, :], start=(rc == 0), stop=(rc == FC - 1)))
                for rc in range(FC)],
                deps=t_attn_w + [tw2, t_pb[oc % 2]])
            t_wout_last = tm
            ttf = P.add("dve", lambda e, oc=oc, bank=bank: e.tensor_tensor(
                tfull[:, oc, :], ln3[:, oc, :], bank[:, 0:512], ALU.add),
                deps=[tm, t_ln3[oc % 2], t_x2_w])
            txu = P.add("dve", lambda e, oc=oc, bank=bank: e.tensor_tensor(
                x2[:, oc, :], x2[:, oc, :], bank[:, 0:512], ALU.add),
                deps=[ttf])
            t_pb[oc % 2] = txu
            t_tf.append(ttf)
            t_x2_rd_new = txu
        t_ln3_rd = t_tf[-1]
        tre2, _, tsr3 = norm_stats(
            ctx, lambda fc: tfull[:, fc, :], None, PB[1][:, 0:512],
            sscr[0], sscr[1], None, rmsb, t_tf + [t_pb[1]], 1e-6, mean=False)
        t_pb[1] = tsr3
        th2 = [None, None]
        t_xm = t_xm_rd
        for fc in range(FC):
            p = fc % 2
            tA = P.add("dve", lambda e, fc=fc, p=p: e.tensor_tensor(
                ascr[p], tfull[:, fc, :], rmsb, ALU.mult),
                deps=[tre2, th2[p]])
            tB = P.add("dve", lambda e, fc=fc, p=p: e.tensor_scalar(
                h2b[:, fc, :], ascr[p], g2s[:, fc:fc + 1], 0.0, ALU.mult,
                ALU.add, accum_out=xmp[:, 0:1]),
                deps=[tA, tg2, t_h2_st, t_xm, t_h1_rd, tnd])
            t_xm = P.add("dve", lambda e, fc=fc: e.tensor_tensor(
                xmean[:, fc:fc + 1], xmean[:, fc:fc + 1], xmp[:, 0:1], ALU.add),
                deps=[tB])
            th2[p] = t_xm
        t_xm_rd = t_xm
        t_h2_st = P.dma("sp", lambda e, tsl=tsl: e.dma_start(
            out=h2r[:, :, tsl], in_=h2b), deps=[th2[0], th2[1]])
        t_x_st = P.dma("sp", lambda e, tsl=tsl: e.dma_start(
            out=xsr[:, :, tsl], in_=x2), deps=[t_x2_rd_new])
        dbg = ctx.get("dbg") or {}
        if dbg:
            t_attn_rd = P.dma("sp", lambda e, tsl=tsl: e.dma_start(
                out=dbg["attn"].rearrange("(c p) t -> p c t", p=128)[:, :, tsl],
                in_=attn), deps=t_attn_w + [t_wout_last])
            t_h1_dmp = P.dma("sp", lambda e, tsl=tsl: e.dma_start(
                out=dbg["h1"].rearrange("(c p) t -> p c t", p=128)[:, :, tsl],
                in_=h1b), deps=t_h1 + [t_wout_last])
            t_qp_dmp = P.dma("sp", lambda e, t=t: e.dma_start(
                out=dbg["qp"].rearrange("p (h l) -> p h l", h=H)
                [:, :, t * 512:(t + 1) * 512], in_=qpad), deps=[tnd])
            t_kp_dmp = P.dma("sp", lambda e, t=t: e.dma_start(
                out=dbg["kp"].rearrange("p (h l) -> p h l", h=H)
                [:, :, t * 512:(t + 1) * 512], in_=kpad), deps=[tnd])
            t_qp_rd = t_qp_dmp
            t_kp_rd = t_kp_dmp
        t_x2_rd = t_x2_rd_new
        t_h1_rd = tnd if not dbg else t_h1_dmp
        t_attn_rd = t_wout_last if not dbg else t_attn_rd
        t_cs_rd[bi] = tnd
    AR.pop()


def emit_ada(ctx):
    P, AR, sc, PB = ctx["P"], ctx["AR"], ctx["sc"], ctx["PB"]
    L = ctx["L"]
    xs, h2s, hids = ctx["xs"], ctx["h2s"], ctx["hids"]
    w_dp1, dp2c, ghc = ctx["w_dp1"], ctx["dp2c"], ctx["ghc"]
    w_gate, w_up, w_down = ctx["w_gate"], ctx["w_up"], ctx["w_down"]
    cc_in, cc_out = ctx["cc_in"], ctx["cc_out"]
    ncores = ctx["ncores"]
    xmean = ctx["xmean"]
    ccsem = ctx["ccsem"]
    xsr = xs.rearrange("(f p) t -> p f t", p=128)
    h2r = h2s.rearrange("(f p) t -> p f t", p=128)
    hidr = hids.rearrange("(c p) t -> p c t", p=128)
    NT = L // 512

    AR.push()
    wdp1 = AR.alloc(FC * 256, BF16).rearrange("p (f j) -> p f j", f=FC)
    dp2b = AR.alloc(2, BF16)
    xmb = AR.alloc(FC, BF16)
    ybf = AR.alloc(2, BF16)
    srow = AR.alloc(4)
    gath = AR.alloc(128)
    szc = AR.alloc(1)
    rsz = AR.alloc(1)
    maskc = AR.alloc(36)
    rinv = AR.alloc(512)
    td1 = P.dma("pool", lambda e: e.dma_start(
        out=wdp1, in_=w_dp1.rearrange("(f p) j -> p f j", p=128)))
    td2 = P.dma("pool", lambda e: e.dma_start(out=dp2b, in_=dp2c[:]))
    tghs = ctx["tghs"] = None
    txm = P.add("dve", lambda e: e.tensor_scalar(
        xmb, xmean, 1.0 / L, None, ALU.mult))
    tmm1 = P.add("pe", [
        (lambda e, fc=fc, j2=j2: e.matmul(
            PB[0][:, j2:j2 + 1], wdp1[:, fc, j2 * 128:(j2 + 1) * 128],
            xmb[:, fc:fc + 1], start=(fc == 0), stop=(fc == FC - 1)))
        for j2 in range(2) for fc in range(FC)], deps=[td1, txm])
    tsil = P.add("act", lambda e: e.activation(ybf[:, 0:2], PB[0][:, 0:2],
                 AF.Silu), deps=[tmm1])
    tmm2 = P.add("pe", [
        (lambda e, j2=j2: e.matmul(
            PB[1][0:1, 0:1], dp2b[:, j2:j2 + 1], ybf[:, j2:j2 + 1],
            start=(j2 == 0), stop=(j2 == 1)))
        for j2 in range(2)], deps=[td2, tsil])
    tsig = P.add("act", lambda e: e.activation(srow[0:1, 0:1], PB[1][0:1, 0:1],
                 AF.Sigmoid), deps=[tmm2])
    tcl = P.add("dve", lambda e: e.tensor_scalar(
        srow[0:1, 0:1], srow[0:1, 0:1], 0.5, 1.5, ALU.add, ALU.min),
        deps=[tsig])
    tcl2 = P.add("dve", lambda e: e.tensor_scalar(
        srow[0:1, 1:2], srow[0:1, 0:1], 0.5, None, ALU.max), deps=[tcl])
    tz = P.add("dve", lambda e: e.memset(gath[0:1, 0:128], 0.0))
    tcp = P.add("dve", lambda e: e.tensor_copy(gath[0:1, 0:1], srow[0:1, 1:2]),
                deps=[tz, tcl2])
    tci = P.dma("pool", lambda e: e.dma_start(out=cc_in[:], in_=gath[0:1, 0:128]),
                deps=[tcp])
    tcc = P.add("pool", lambda e: e.collective_compute(
        "AllGather", ALU.bypass, replica_groups=[list(range(ncores))],
        ins=[cc_in[:]], outs=[cc_out[:]]).then_inc(ccsem, 1), deps=[tci],
        tick=False)
    tcw = P.add("pool", [lambda e: e.wait_ge(ccsem, 1), lambda e: e.nop()])
    tco = P.dma("pool", lambda e: e.dma_start(
        out=gath[0:ncores, 0:128], in_=cc_out[:]), deps=[tcw])
    tsm = P.add("pe", lambda e: e.matmul(
        PB[2][0:1, 0:1], sc["ones_f"][0:ncores, 0:1], gath[0:ncores, 0:1],
        start=True, stop=True), deps=[tco, tcp])
    ts1 = P.add("dve", lambda e: e.tensor_scalar(
        srow[0:1, 2:3], PB[2][0:1, 0:1], 3072.0 / ncores, None, ALU.mult),
        deps=[tsm])
    ts2 = P.add("dve", lambda e: e.tensor_scalar(
        srow[0:1, 2:3], srow[0:1, 2:3], 0.5, MAGIC, ALU.subtract, ALU.add),
        deps=[ts1])
    ts3 = P.add("dve", lambda e: e.tensor_scalar(
        srow[0:1, 2:3], srow[0:1, 2:3], MAGIC, 1.0, ALU.subtract, ALU.max),
        deps=[ts2])
    trz = P.add("dve", lambda e: e.reciprocal(rsz[0:1, 0:1], srow[0:1, 2:3]),
                deps=[ts3])
    tsb = P.add("pe", lambda e: e.matmul(
        PB[3][:, 0:1], sc["ones_f"][0:1, :], srow[0:1, 2:3],
        start=True, stop=True), deps=[ts3, tcl2])
    tsc = P.add("dve", lambda e: e.tensor_copy(szc, PB[3][:, 0:1]), deps=[tsb])
    tmk = P.add("dve", lambda e: e.tensor_scalar(
        maskc, sc["iota36"], szc[:, 0:1], -1.0, ALU.subtract, ALU.mult),
        deps=[tsc])
    tmk2 = P.add("dve", lambda e: e.tensor_scalar(
        maskc, maskc, 1.0, 0.0, ALU.min, ALU.max), deps=[tmk])

    rbst = AR.alloc(NT * 512, BF16).rearrange("p (t c) -> p t c", t=NT)
    AR.push()
    wg = AR.alloc(FC * FADA, BF16).rearrange("p (f c) -> p f c", f=FC)
    wu = AR.alloc(FC * FADA, BF16).rearrange("p (f c) -> p f c", f=FC)
    h2t = [AR.alloc(FC * 512, BF16).rearrange("p (f t) -> p f t", f=FC)
           for _ in range(2)]
    gsb = [AR.alloc(512, BF16) for _ in range(2)]
    hvs = [AR.alloc(512) for _ in range(2)]
    hbt = [AR.alloc(512, BF16) for _ in range(2)]
    sqb = [AR.alloc(512, BF16) for _ in range(3)]
    rrow2 = AR.alloc(512)
    twg = P.dma("pool", lambda e: e.dma_start(
        out=wg, in_=w_gate.rearrange("(f p) c -> p f c", p=128)))
    twu = P.dma("pool", lambda e: e.dma_start(
        out=wu, in_=w_up.rearrange("(f p) c -> p f c", p=128)))
    t_h2_rd = [None, None]
    t_pb = {i: None for i in range(8)}
    t_pb[0] = tsil
    t_pb[1] = tsig
    t_pb[2] = ts1
    t_pb[3] = tsc
    t_gsb = [None, None]
    t_hvs = [None, None]
    t_hbt = [None, None]
    t_hbt_sq = [None, None]
    t_sqb = [None, None, None]
    t_rr = None
    t_rb_rd = None
    for t in range(NT):
        bi = t % 2
        tsl = slice(t * 512, (t + 1) * 512)
        tl = P.dma("sp", lambda e, bi=bi, tsl=tsl: e.dma_start(
            out=h2t[bi], in_=h2r[:, :, tsl]), deps=[t_h2_rd[bi]])
        t_sq_last = None
        t_sqq = []
        for dc in range(36):
            gb = dc % 2
            ub = 2 + dc % 2
            tg = P.add("pe", [
                (lambda e, fc=fc, dc=dc, bi=bi, gb=gb: e.matmul(
                    PB[gb][:, 0:512], wg[:, fc, dc * 128:(dc + 1) * 128],
                    h2t[bi][:, fc, :], start=(fc == 0), stop=(fc == FC - 1)))
                for fc in range(FC)], deps=[twg, tl, t_pb[gb]])
            tu = P.add("pe", [
                (lambda e, fc=fc, dc=dc, bi=bi, ub=ub: e.matmul(
                    PB[ub][:, 0:512], wu[:, fc, dc * 128:(dc + 1) * 128],
                    h2t[bi][:, fc, :], start=(fc == 0), stop=(fc == FC - 1)))
                for fc in range(FC)], deps=[twu, tl, t_pb[ub]])
            tsi = P.add("act", lambda e, gb=gb: e.activation(
                gsb[gb], PB[gb][:, 0:512], AF.Silu), deps=[tg, t_gsb[gb]])
            t_pb[gb] = tsi
            thv = P.add("dve", lambda e, gb=gb, ub=ub: e.tensor_tensor(
                hvs[gb], PB[ub][:, 0:512], gsb[gb], ALU.mult),
                deps=[tu, tsi, t_hvs[gb]])
            t_gsb[gb] = thv
            t_pb[ub] = thv
            thb = P.add("dve", lambda e, gb=gb, dc=dc: e.tensor_scalar(
                hbt[gb], hvs[gb], maskc[:, dc:dc + 1], None, ALU.mult),
                deps=[thv, tmk2, t_hbt[gb], t_hbt_sq[gb]])
            t_hvs[gb] = thb
            tst = P.dma("sp", lambda e, gb=gb, dc=dc, tsl=tsl: e.dma_start(
                out=hidr[:, dc, tsl], in_=hbt[gb]), deps=[thb])
            t_hbt[gb] = tst
            s3 = dc % 3
            tsq = P.add("act", lambda e, gb=gb, s3=s3: e.activation(
                sqb[s3], hbt[gb], AF.Square), deps=[thb, t_sqb[s3]])
            t_hbt_sq[gb] = tsq
            t_sqq.append((dc, s3, tsq))
            if dc >= 2:
                d2, s32, tsq2 = t_sqq.pop(0)
                t_sq_last = P.add("pe", lambda e, s32=s32, d2=d2: e.matmul(
                    PB[4][0:1, 0:512], sc["ones_b"][:, 0:1], sqb[s32],
                    start=(d2 == 0), stop=(d2 == 35)),
                    deps=[tsq2, t_sq_last, t_rr if d2 == 0 else None])
                t_sqb[s32] = t_sq_last
        while t_sqq:
            d2, s32, tsq2 = t_sqq.pop(0)
            t_sq_last = P.add("pe", lambda e, s32=s32, d2=d2: e.matmul(
                PB[4][0:1, 0:512], sc["ones_b"][:, 0:1], sqb[s32],
                start=(d2 == 0), stop=(d2 == 35)),
                deps=[tsq2, t_sq_last, t_rr if d2 == 0 else None])
            t_sqb[s32] = t_sq_last
        tr1 = P.add("dve", lambda e: e.tensor_scalar(
            rrow2[0:1, 0:512], PB[4][0:1, 0:512], rsz[0:1, 0:1], None,
            ALU.mult), deps=[t_sq_last, trz])
        t_rr = tr1
        tr2 = P.add("act", lambda e: e.activation(
            rrow2[0:1, 0:512], rrow2[0:1, 0:512], AF.Ln,
            bias=sc["cvals"][0:1, 1:2]), deps=[tr1])
        tr3 = P.add("act", lambda e: e.activation(
            rrow2[0:1, 0:512], rrow2[0:1, 0:512], AF.Exp, scale=-0.5),
            deps=[tr2])
        trb = P.add("pe", lambda e: e.matmul(
            PB[5][:, 0:512], sc["ones_f"][0:1, :], rrow2[0:1, 0:512],
            start=True, stop=True), deps=[tr3, t_pb[5]])
        trc = P.add("dve", lambda e, t=t: e.tensor_copy(
            rbst[:, t, :], PB[5][:, 0:512]), deps=[trb, t_rb_rd])
        t_pb[5] = trc
        t_h2_rd[bi] = tu
    P.barrier()
    AR.pop()

    wd = AR.alloc(36 * D, BF16).rearrange("p (c o) -> p c o", c=36)
    ghs = AR.alloc(36)
    x_half = AR.alloc(FC * (L // 2)).rearrange("p (f t) -> p f t", f=FC)
    hidc = [AR.alloc(512, BF16) for _ in range(2)]
    hrb = [AR.alloc(512, BF16) for _ in range(2)]
    twd = P.dma("pool", lambda e: e.dma_start(
        out=wd, in_=w_down.rearrange("(c p) o -> p c o", p=128)))
    tgh = P.dma("sp", lambda e: e.dma_start(out=ghs, in_=ghc[:]))
    tfold = None
    for dc in range(36):
        tfold = P.add("dve", lambda e, dc=dc: e.tensor_scalar(
            wd[:, dc, :], wd[:, dc, :], ghs[:, dc:dc + 1], None, ALU.mult),
            deps=[twd, tgh, tfold])
    t_x_st = None
    t_hid_rd = [None, None]
    t_hrb_rd = [None, None]
    LH = L // 2
    for half in range(2):
        hsl = slice(half * LH, (half + 1) * LH)
        tlx = P.dma("sp", lambda e, hsl=hsl: e.dma_start(
            out=x_half, in_=xsr[:, :, hsl]), deps=[t_x_st])
        t_add_last = None
        for t4 in range(LH // 512):
            gt = half * (LH // 512) + t4
            tsl = slice(t4 * 512, (t4 + 1) * 512)
            t_mm_last = None
            for dc in range(36):
                hb = dc % 2
                tlh = P.dma("sp", lambda e, hb=hb, dc=dc, gt=gt: e.dma_start(
                    out=hidc[hb], in_=hidr[:, dc, gt * 512:(gt + 1) * 512]),
                    deps=[t_hid_rd[hb]])
                thr = P.add("dve", lambda e, hb=hb, gt=gt: e.tensor_tensor(
                    hrb[hb], hidc[hb], rbst[:, gt, :], ALU.mult),
                    deps=[tlh, t_hrb_rd[hb]])
                t_hid_rd[hb] = thr
                tmm = P.add("pe", [
                    (lambda e, oc=oc, dc=dc, hb=hb: e.matmul(
                        PB[oc][:, 0:512], wd[:, dc, oc * 128:(oc + 1) * 128],
                        hrb[hb], start=(dc == 0), stop=(dc == 35)))
                    for oc in range(FC)],
                    deps=[thr, tfold, t_mm_last, t_add_last if dc == 0 else None])
                t_hrb_rd[hb] = tmm
                t_mm_last = tmm
            prev = [t_mm_last]
            for oc in range(FC):
                ta = P.add("dve", lambda e, oc=oc, t4=t4, tsl=tsl: e.tensor_tensor(
                    x_half[:, oc, tsl], x_half[:, oc, tsl], PB[oc][:, 0:512],
                    ALU.add), deps=prev + [tlx])
                prev = [ta]
            t_add_last = prev[0]
        t_x_st = P.dma("sp", lambda e, hsl=hsl: e.dma_start(
            out=xsr[:, :, hsl], in_=x_half), deps=[t_add_last])
    AR.pop()


def build(L, phases, ncores=NCORE):
    nc = bass.Bass("TRN2", target_bir_lowering=False)
    NT512 = L // 512
    LH = L // 2
    NHT = LH // 512

    x_in = nc.dram_tensor("x_in", [L, D], F32, kind="ExternalInput")
    aW = nc.dram_tensor("aW", [D, 3 * D], F32, kind="ExternalInput")
    w_qkv = nc.dram_tensor("w_qkv", [D, 3 * D], F32, kind="ExternalInput")
    w_out = nc.dram_tensor("w_out", [D, D], F32, kind="ExternalInput")
    g1c = nc.dram_tensor("g1c", [128, FC], F32, kind="ExternalInput")
    g2c = nc.dram_tensor("g2c", [128, FC], F32, kind="ExternalInput")
    w_dp1 = nc.dram_tensor("w_dp1", [D, 256], F32, kind="ExternalInput")
    dp2c = nc.dram_tensor("dp2c", [128, 2], F32, kind="ExternalInput")
    w_gate = nc.dram_tensor("w_gate", [D, FADA], F32, kind="ExternalInput")
    w_up = nc.dram_tensor("w_up", [D, FADA], F32, kind="ExternalInput")
    w_down = nc.dram_tensor("w_down", [FADA, D], F32, kind="ExternalInput")
    ghc = nc.dram_tensor("ghc", [128, 36], F32, kind="ExternalInput")
    fW1 = nc.dram_tensor("fW1", [E, D, DFF], F32, kind="ExternalInput")
    b1c = nc.dram_tensor("b1c", [128, E * 32], F32, kind="ExternalInput")
    fW2 = nc.dram_tensor("fW2", [E, DFF, D], F32, kind="ExternalInput")
    fb2 = nc.dram_tensor("fb2", [E, D], F32, kind="ExternalInput")
    y_out = nc.dram_tensor("y_out", [L, D], F32, kind="ExternalOutput")

    CONSTS = host_constants(L)
    cin = {}
    for k, v in CONSTS.items():
        dt = BF16 if v.dtype == ml_dtypes.bfloat16 else F32
        cin[k] = nc.dram_tensor("c_" + k, list(v.shape), dt, kind="ExternalInput")

    xs = nc.dram_tensor("xs", [D, L], F32)
    tsd = nc.dram_tensor("tsd", [D, L], F32)
    ln1s = nc.dram_tensor("ln1s", [D, L], BF16)
    ktvs = nc.dram_tensor("ktvs", [L, 2 * D], BF16)
    h2s = nc.dram_tensor("h2s", [D, L], BF16)
    hids = nc.dram_tensor("hids", [FADA, L], BF16)
    cc_in = nc.dram_tensor("cc_in", [1, 128], F32)
    cc_out = nc.dram_tensor("cc_out", [ncores, 128], F32, addr_space="Shared")
    KDBG = os.environ.get("KDBG", "") == "1"
    dbg = {}
    if KDBG:
        dbg["ln2"] = nc.dram_tensor("dbg_ln2", [D, L], BF16, kind="ExternalOutput")
        dbg["m8"] = nc.dram_tensor("dbg_m8", [1, L], F32, kind="ExternalOutput")
        dbg["wm8"] = nc.dram_tensor("dbg_wm8", [8, L], BF16, kind="ExternalOutput")
        dbg["xpre"] = nc.dram_tensor("dbg_xpre", [D, L], F32, kind="ExternalOutput")
        dbg["attn"] = nc.dram_tensor("dbg_attn", [D, L], BF16, kind="ExternalOutput")
        dbg["h1"] = nc.dram_tensor("dbg_h1", [D, L], BF16, kind="ExternalOutput")
        dbg["qp"] = nc.dram_tensor("dbg_qp", [128, H * L], BF16, kind="ExternalOutput")
        dbg["kp"] = nc.dram_tensor("dbg_kp", [128, H * L], BF16, kind="ExternalOutput")
        dbg["den0"] = nc.dram_tensor("dbg_den0", [1, 256], F32, kind="ExternalOutput")
        dbg["num0"] = nc.dram_tensor("dbg_num0", [64, 256], F32, kind="ExternalOutput")
        dbg["abf0"] = nc.dram_tensor("dbg_abf0", [128, 512], BF16, kind="ExternalOutput")
        dbg["qmm"] = nc.dram_tensor("dbg_qmm", [128, 256], BF16, kind="ExternalOutput")
        dbg["kmm"] = nc.dram_tensor("dbg_kmm", [128, 128], BF16, kind="ExternalOutput")
        dbg["pb0"] = nc.dram_tensor("dbg_pb0", [128, 512], F32, kind="ExternalOutput")

    from contextlib import ExitStack
    es = ExitStack()
    AR = Arena(es.enter_context(nc.sbuf_tensor("AR", [128, 51712], F32)), 51712)
    SM = Arena(es.enter_context(nc.sbuf_tensor("SM", [128, 1280], F32)), 1280)

    sc = {}
    for k, v in CONSTS.items():
        if k in ("cos2", "sin2"):
            continue
        p, n = v.shape
        dtype = BF16 if v.dtype == ml_dtypes.bfloat16 else F32
        ncols_f32 = n if dtype == F32 else (n + 1) // 2
        a = SM if SM.off + ncols_f32 <= SM.cols else AR
        sc[k] = a.alloc(n, dtype)[:p, :]

    b2pk = AR.alloc(D, BF16)[:8, :]
    xmean = AR.alloc(FC)

    PB = [es.enter_context(nc.psum_tensor(f"PB{i}", [128, 512], F32))
          for i in range(8)]

    block = es.enter_context(nc.Block())
    P = Prog(nc)
    P.make_sems()
    ccsem = es.enter_context(nc.semaphore("ccsem"))

    ctx = dict(nc=nc, P=P, AR=AR, SM=SM, sc=sc, PB=PB, L=L, NT512=NT512,
               ncores=ncores, ccsem=ccsem,
               LH=LH, NHT=NHT, NCH=L // CHUNK,
               x_in=x_in, aW=aW, w_qkv=w_qkv, w_out=w_out, g1c=g1c, g2c=g2c,
               w_dp1=w_dp1, dp2c=dp2c, w_gate=w_gate, w_up=w_up,
               w_down=w_down, ghc=ghc, fW1=fW1, b1c=b1c, fW2=fW2, fb2=fb2,
               y_out=y_out, CONSTS=CONSTS, cin=cin, xs=xs, tsd=tsd, dbg=dbg,
               ln1s=ln1s, ktvs=ktvs, h2s=h2s, hids=hids, cc_in=cc_in,
               cc_out=cc_out, b2pk=b2pk, xmean=xmean)

    for k, v in CONSTS.items():
        if k in ("cos2", "sin2"):
            continue
        src, dst = cin[k], sc[k]
        P.dma("sp", lambda e, s=src, d=dst: e.dma_start(out=d, in_=s[:]))
    ctx["t_b2pk"] = P.dma("pool", lambda e: e.dma_start(out=b2pk, in_=fb2[:]))
    P.barrier()

    emit_tin(ctx)
    P.barrier()
    if "attna" in phases:
        emit_attna(ctx)
        P.barrier()
    if "moe1" in phases:
        emit_moe(ctx)
        P.barrier()
    if "attnb" in phases:
        emit_attnb(ctx)
        P.barrier()
    if "ada" in phases:
        emit_ada(ctx)
        P.barrier()
    if "moe2" in phases:
        emit_moe(ctx)
        P.barrier()
    emit_tout(ctx)
    P.barrier()
    P.add("sp", lambda e: e.nop(), deps=P.final_waits(), tick=False)
    P.build(block)
    return nc, es


_CACHE = {}
_LAST_RES = None


def kernel(L0_attn_a_x, L0_attn_a_W, L0_attn_b_w_qkv, L0_attn_b_w_out,
           L0_attn_b_g1, L0_attn_b_g2, L0_attn_b_w_dp1, L0_attn_b_w_dp2,
           L0_attn_b_w_gate, L0_attn_b_w_up, L0_attn_b_w_down,
           L0_attn_b_g_hidden, L0_ffn_W1, L0_ffn_b1, L0_ffn_W2, L0_ffn_b2):
    x = np.ascontiguousarray(np.asarray(L0_attn_a_x, dtype=np.float32))
    B, L, _ = x.shape
    phases = tuple(p for p in os.environ.get(
        "KPHASES", "attna,moe1,attnb,ada,moe2").split(",") if p)
    key = (L, phases, NCORE)
    if key not in _CACHE:
        _CACHE[key] = build(L, phases, NCORE)
    nc, _es = _CACHE[key]

    consts = host_constants(L)
    common = {
        "aW": np.ascontiguousarray(L0_attn_a_W, dtype=np.float32),
        "w_qkv": np.ascontiguousarray(L0_attn_b_w_qkv, dtype=np.float32),
        "w_out": np.ascontiguousarray(L0_attn_b_w_out, dtype=np.float32),
        "g1c": np.ascontiguousarray(np.asarray(L0_attn_b_g1, np.float32).reshape(FC, 128).T),
        "g2c": np.ascontiguousarray(np.asarray(L0_attn_b_g2, np.float32).reshape(FC, 128).T),
        "w_dp1": np.ascontiguousarray(L0_attn_b_w_dp1, dtype=np.float32),
        "dp2c": np.ascontiguousarray(np.asarray(L0_attn_b_w_dp2, np.float32).reshape(2, 128).T),
        "w_gate": np.ascontiguousarray(L0_attn_b_w_gate, dtype=np.float32),
        "w_up": np.ascontiguousarray(L0_attn_b_w_up, dtype=np.float32),
        "w_down": np.ascontiguousarray(L0_attn_b_w_down, dtype=np.float32),
        "ghc": np.ascontiguousarray(np.asarray(L0_attn_b_g_hidden, np.float32).reshape(36, 128).T),
        "fW1": np.ascontiguousarray(L0_ffn_W1, dtype=np.float32),
        "b1c": np.ascontiguousarray(
            np.asarray(L0_ffn_b1, np.float32).reshape(E, 32, 128)
            .transpose(2, 0, 1).reshape(128, E * 32)),
        "fW2": np.ascontiguousarray(L0_ffn_W2, dtype=np.float32),
        "fb2": np.ascontiguousarray(L0_ffn_b2, dtype=np.float32),
    }
    for k, v in consts.items():
        common["c_" + k] = np.ascontiguousarray(v)
    in_maps = []
    for c in range(NCORE):
        m = dict(common)
        m["x_in"] = np.ascontiguousarray(x[c])
        in_maps.append(m)
    res = run_bass_kernel_spmd(nc, in_maps, core_ids=list(range(NCORE)))
    global _LAST_RES
    _LAST_RES = res
    out = np.stack([np.asarray(res.results[c]["y_out"]) for c in range(NCORE)],
                   axis=0)
    return out.astype(np.float32)
